# version 17
# baseline (speedup 1.0000x reference)
import numpy as np
import ml_dtypes
from contextlib import ExitStack

import concourse.bass as bass
import concourse.mybir as mybir
from concourse.bass import IndirectOffsetOnAxis
from concourse.bass_utils import run_bass_kernel_spmd

F32 = mybir.dt.float32
BF16 = mybir.dt.bfloat16
I32 = mybir.dt.int32
U32 = mybir.dt.uint32
AF = mybir.ActivationFunctionType
ALU = mybir.AluOpType
AX = mybir.AxisListType

NCORES = 8
NSEQ = 4
SEQ = 2048
DM = 1024
H = 16
WS = 8
T = NSEQ * SEQ
NBLK = 16
NSUB = 64
NE = 8
CLOC = 1280
TRASH = NE * CLOC
NTI = CLOC // 128
EPS = 1e-5

ENGS = ("sync", "vector", "scalar", "tensor", "gpsimd")


class Sched:

    def __init__(self, nc, stack):
        self.nc = nc
        self.stack = stack
        self.ops = {e: [] for e in ENGS}
        self.sems = {}
        self.cnt = {}
        for e in ENGS:
            key = "eng:" + e
            self.sems[key] = stack.enter_context(nc.semaphore(f"prog_{e}"))
            self.cnt[key] = 0
        self.eng_of_sem = {("eng:" + e): e for e in ENGS}
        self.wr = {}
        self.rd = {}
        self.streams = {}

    def stream(self, name, eng, k=4):
        slots = []
        for i in range(k):
            key = f"st:{name}:{i}"
            self.sems[key] = self.stack.enter_context(self.nc.semaphore(f"s_{name}_{i}"))
            self.cnt[key] = 0
            slots.append(key)
        self.streams[name] = {"eng": eng, "slots": slots, "i": 0}
        return name

    def _deps(self, eng, reads, writes, extra):
        deps = {}

        def need(tok):
            if tok is None:
                return
            dk, dv = tok
            if dv > deps.get(dk, 0):
                deps[dk] = dv

        for k in reads:
            need(self.wr.get(k))
        for k in writes:
            need(self.wr.get(k))
            for dk, dv in self.rd.get(k, {}).items():
                need((dk, dv))
        for tok in extra:
            need(tok)
        return deps

    def _book(self, tok, reads, writes):
        for k in reads:
            self.rd.setdefault(k, {})[tok[0]] = tok[1]
        for k in writes:
            self.wr[k] = tok
            self.rd[k] = {}
        return tok

    def op(self, eng, fn, reads=(), writes=(), extra=()):
        deps = self._deps(eng, reads, writes, extra)
        key = "eng:" + eng
        self.ops[eng].append((sorted(deps.items()), fn, (key, 1)))
        self.cnt[key] += 1
        return self._book((key, self.cnt[key]), reads, writes)

    def dma(self, stream, fn, reads=(), writes=(), extra=()):
        st = self.streams[stream]
        eng = st["eng"]
        key = st["slots"][st["i"] % len(st["slots"])]
        st["i"] += 1
        deps = self._deps(eng, reads, writes, extra)
        if self.cnt[key] > deps.get(key, 0):
            deps[key] = self.cnt[key]
        self.ops[eng].append((sorted(deps.items()), fn, (key, 16)))
        self.cnt[key] += 16
        return self._book((key, self.cnt[key]), reads, writes)

    def total(self, stream):
        st = self.streams[stream]
        return [(k, self.cnt[k]) for k in st["slots"] if self.cnt[k] > 0]

    def emit(self, block):
        for eng in ENGS:
            ops = self.ops[eng]
            if not ops:
                continue

            def make(eng, ops):
                def body(e):
                    hi = {}
                    for deps, fn, inc in ops:
                        for dk, dv in deps:
                            if dv > hi.get(dk, 0):
                                e.wait_ge(self.sems[dk], dv)
                                hi[dk] = dv
                        ik, iv = inc
                        fn(e).then_inc(self.sems[ik], iv)
                return body

            getattr(block, eng)(make(eng, ops))


def _rolled_pieces(j):
    if j < 15:
        return [(0, 4 + 128 * j, 128)]
    return [(0, 1924, 124), (124, 0, 4)]



L1_PHASES = "AB"
L1_STAGE = 99


def build_l1(debug=False):
    nc = bass.Bass("TRN2", target_bir_lowering=False)
    dbg = {}
    if debug:
        for name, shape, dt in (("d_qfm", [128, 4096], BF16), ("d_kfm", [128, 4096], BF16),
                                ("d_vbf", [128, 1024], BF16), ("d_ssb", [128, 512], F32),
                                ("d_pbf", [128, 512], BF16), ("d_ptbf", [128, 512], BF16),
                                ("d_atf", [128, 4096], BF16), ("d_y", [128, 1024], F32)):
            dbg[name] = nc.dram_tensor(name, shape, dt, kind="ExternalOutput").ap()

    x_in = nc.dram_tensor("x", [T, DM], F32, kind="ExternalInput").ap()
    wqkv_in = nc.dram_tensor("wqkv", [DM, 3 * DM], F32, kind="ExternalInput").ap()
    bqkv_in = nc.dram_tensor("bqkv", [3 * DM], F32, kind="ExternalInput").ap()
    wproj_in = nc.dram_tensor("wproj", [DM, DM], F32, kind="ExternalInput").ap()
    bproj_in = nc.dram_tensor("bproj", [DM], F32, kind="ExternalInput").ap()
    mb0_in = nc.dram_tensor("mb0", [128, 256], F32, kind="ExternalInput").ap()
    mb1_in = nc.dram_tensor("mb1", [128, 256], F32, kind="ExternalInput").ap()
    wr_in = nc.dram_tensor("wr", [DM, NE], F32, kind="ExternalInput").ap()

    x1f_out = nc.dram_tensor("x1f", [T, DM], F32, kind="ExternalOutput").ap()
    x1b_out = nc.dram_tensor("x1b", [T, DM], BF16, kind="ExternalOutput").ap()
    eidx_out = nc.dram_tensor("eidx", [128, NSUB], F32, kind="ExternalOutput").ap()
    p_out = nc.dram_tensor("ptok", [128, NSUB], F32, kind="ExternalOutput").ap()
    rankl_out = nc.dram_tensor("rankl", [128, NSUB], F32, kind="ExternalOutput").ap()
    tot_out = nc.dram_tensor("tot", [1, NE], F32, kind="ExternalOutput").ap()

    cnt_dram = nc.dram_tensor("cnt_scr", [NSUB, NE], F32, kind="Internal")
    to_dram = nc.dram_tensor("to_scr", [NSUB, NE], F32, kind="Internal")

    ident_bf_c = nc.inline_tensor(np.eye(128, dtype=ml_dtypes.bfloat16), "ident_bf")
    ident_f_c = nc.inline_tensor(np.eye(128, dtype=np.float32), "ident_f")
    l128_c = nc.inline_tensor(np.triu(np.ones((128, 128), dtype=np.float32), 1), "l128")
    onesc_c = nc.inline_tensor(np.ones((128, 1), dtype=np.float32), "onesc")
    onesrf_c = nc.inline_tensor(np.ones((1, 128), dtype=np.float32), "onesrf")
    onesrb_c = nc.inline_tensor(np.ones((1, 128), dtype=ml_dtypes.bfloat16), "onesrb")
    iota8_c = nc.inline_tensor(np.tile(np.arange(8, dtype=np.float32)[None, :], (128, 1)), "iota8")

    with ExitStack() as st:
        en = st.enter_context
        sc = Sched(nc, st)
        sc.stream("cst", "sync", 2)
        sc.stream("xl", "sync", 4)
        sc.stream("x1f", "sync", 4)
        sc.stream("bnc", "sync", 2)
        sc.stream("x1b", "scalar", 4)
        sc.stream("wl", "gpsimd", 4)
        sc.stream("x1r", "gpsimd", 4)

        wqkv_bf = en(nc.sbuf_tensor("wqkv_bf", [128, 8 * 3072], BF16))
        wproj_bf = en(nc.sbuf_tensor("wproj_bf", [128, 8 * 1024], BF16))
        wr_bf = en(nc.sbuf_tensor("wr_bf", [128, 8 * 8], BF16))
        bqkv_sb = en(nc.sbuf_tensor("bqkv_sb", [128, 24], F32))
        bq8_sb = en(nc.sbuf_tensor("bq8_sb", [128, 8], F32))
        bproj_row = en(nc.sbuf_tensor("bproj_row", [1, DM], BF16))
        bv_row = en(nc.sbuf_tensor("bv_row", [1, DM], BF16))
        mb0_sb = en(nc.sbuf_tensor("mb0_sb", [128, 256], F32))
        mb1_sb = en(nc.sbuf_tensor("mb1_sb", [128, 256], F32))
        idb_sb = en(nc.sbuf_tensor("idb_sb", [128, 128], BF16))
        idf_sb = en(nc.sbuf_tensor("idf_sb", [128, 128], F32))
        l128_sb = en(nc.sbuf_tensor("l128_sb", [128, 128], F32))
        onec_sb = en(nc.sbuf_tensor("onec_sb", [128, 1], F32))
        onerf_sb = en(nc.sbuf_tensor("onerf_sb", [1, 128], F32))
        onerb_sb = en(nc.sbuf_tensor("onerb_sb", [1, 128], BF16))
        iota8_sb = en(nc.sbuf_tensor("iota8_sb", [128, 8], F32))

        O_all = en(nc.sbuf_tensor("O_all", [128, NSUB * 8], F32))
        p_all = en(nc.sbuf_tensor("p_all", [128, NSUB], F32))
        eidx_all = en(nc.sbuf_tensor("eidx_all", [128, NSUB], F32))

        rankl_all = en(nc.sbuf_tensor("rankl_all", [128, NSUB], F32))
        cnt_row = en(nc.sbuf_tensor("cnt_row", [1, NSUB * 8], F32))
        cnt64 = en(nc.sbuf_tensor("cnt64", [NSUB, 8], F32))
        to64 = en(nc.sbuf_tensor("to64", [NSUB, 8], F32))
        to_row = en(nc.sbuf_tensor("to_row", [1, NSUB * 8], F32))
        tot_row = en(nc.sbuf_tensor("tot_row", [1, 8], F32))

        x_f32 = en(nc.sbuf_tensor("x_f32", [128, 8 * DM], F32))
        xtok_bf = en(nc.sbuf_tensor("xtok_bf", [128, 2 * DM], BF16))
        x_fm = en(nc.sbuf_tensor("x_fm", [128, 2 * 8 * 512], BF16))
        q_fm = en(nc.sbuf_tensor("q_fm", [128, 8 * 512], BF16))
        k_fm = en(nc.sbuf_tensor("k_fm", [128, 8 * 512], BF16))
        v_bf = en(nc.sbuf_tensor("v_bf", [128, 4 * DM], BF16))
        S_sb = en(nc.sbuf_tensor("S_sb", [128, 3 * 512], F32))
        trl_sb = S_sb[:, 0:512]
        P_bf = en(nc.sbuf_tensor("P_bf", [128, 3 * 512], BF16))
        Pt_bf = en(nc.sbuf_tensor("Pt_bf", [128, 3 * 512], BF16))
        RL_all = P_bf[:, 0:1024].bitcast(F32)
        negmx = en(nc.sbuf_tensor("negmx", [128, 3], F32))
        dn_sb = en(nc.sbuf_tensor("dn_sb", [128, 48], F32))
        r_sb = en(nc.sbuf_tensor("r_sb", [128, 48], F32))
        attn_fm = en(nc.sbuf_tensor("attn_fm", [128, 8 * 512], BF16))
        y_f32 = en(nc.sbuf_tensor("y_f32", [128, 2 * DM], F32))
        x1f_sb = en(nc.sbuf_tensor("x1f_sb", [128, 2 * DM], F32))
        x1b_sb = en(nc.sbuf_tensor("x1b_sb", [128, 2 * DM], BF16))
        ln_sm = en(nc.sbuf_tensor("ln_sm", [128, 16], F32))
        eps_sb = en(nc.sbuf_tensor("eps_sb", [128, 1], F32))
        x1r_bf = xtok_bf
        x1_fm = q_fm
        lg_sb = k_fm[0:8, 0:1024].bitcast(F32)
        ltok_sb = en(nc.sbuf_tensor("ltok_sb", [128, 2 * 8], F32))
        nm8_sb = en(nc.sbuf_tensor("nm8_sb", [128, 2], F32))
        e8_sb = en(nc.sbuf_tensor("e8_sb", [128, 2 * 8], F32))
        s8_sb = en(nc.sbuf_tensor("s8_sb", [128, 2], F32))
        mx8_sb = en(nc.sbuf_tensor("mx8_sb", [128, 2 * 8], F32))
        mi8_sb = en(nc.sbuf_tensor("mi8_sb", [128, 2 * 8], U32))

        TX = en(nc.psum_tensor("TX", [128, 1024], BF16))
        PB12 = en(nc.psum_tensor("PB12", [128, 1024], F32))
        PB34 = en(nc.psum_tensor("PB34", [128, 1024], F32))
        PB5 = en(nc.psum_tensor("PB5", [128, 512], F32))
        PB6 = en(nc.psum_tensor("PB6", [128, 512], F32))
        PB7 = en(nc.psum_tensor("PB7", [128, 512], F32))

        block = en(nc.Block())

        for dst, src, key in ((idb_sb, ident_bf_c, "idb"), (idf_sb, ident_f_c, "idf"),
                              (l128_sb, l128_c, "l128"), (onec_sb, onesc_c, "onec"),
                              (onerf_sb, onesrf_c, "onerf"), (onerb_sb, onesrb_c, "onerb"),
                              (iota8_sb, iota8_c, "iota8")):
            sc.dma("cst", lambda e, d=dst, s=src: e.dma_start(out=d[:], in_=s.ap()),
                   writes=[key])
        sc.dma("cst", lambda e: e.dma_start(out=mb0_sb[:], in_=mb0_in), writes=["mb0"])
        sc.dma("cst", lambda e: e.dma_start(out=mb1_sb[:], in_=mb1_in), writes=["mb1"])
        for dc in range(8):
            sc.dma("wl", lambda e, dc=dc: e.dma_start(
                out=wqkv_bf[:, dc * 3072:(dc + 1) * 3072],
                in_=wqkv_in[dc * 128:(dc + 1) * 128, :]), writes=[f"wq{dc}"])
        for c in range(8):
            sc.dma("wl", lambda e, c=c: e.dma_start(
                out=wproj_bf[:, c * 1024:(c + 1) * 1024],
                in_=wproj_in[c * 128:(c + 1) * 128, :]), writes=[f"wp{c}"])
        for dc in range(8):
            sc.dma("wl", lambda e, dc=dc: e.dma_start(
                out=wr_bf[:, dc * 8:(dc + 1) * 8],
                in_=wr_in[dc * 128:(dc + 1) * 128, :]), writes=[f"wr{dc}"])
        for c in range(24):
            sc.dma("wl", lambda e, c=c: e.dma_start(
                out=bqkv_sb[:, c:c + 1],
                in_=bqkv_in[c * 128:(c + 1) * 128].rearrange("(p o) -> p o", o=1)),
                writes=[f"bq{c}"])
        sc.dma("wl", lambda e: e.dma_start(out=bproj_row[:], in_=bproj_in[None, :]),
               writes=["bp"])
        sc.dma("wl", lambda e: e.dma_start(out=bv_row[:], in_=bqkv_in[2048:3072][None, :]),
               writes=["bv"])
        sc.op("vector", lambda e: e.tensor_scalar_mul(bq8_sb[:], bqkv_sb[:, 0:8], 0.125),
              reads=[f"bq{c}" for c in range(8)], writes=["bq8"])
        sc.op("vector", lambda e: e.memset(eps_sb[:], EPS), writes=["eps"])
        sc.op("vector", lambda e: e.memset(O_all[:], 0.0), writes=["O_all"])
        sc.op("vector", lambda e: e.memset(eidx_all[:], 0.0), writes=["eidx_all"])
        sc.op("vector", lambda e: e.memset(p_all[:], 0.0), writes=["p_all"])

        def stage_load(blk):
            seq = blk // 4
            for s in range(4):
                sg = blk * 4 + s
                j = sg % 16
                slot = sg % 8
                for dst0, src0, n in _rolled_pieces(j):
                    sc.dma("xl", lambda e, slot=slot, dst0=dst0, src0=src0, n=n, seq=seq:
                           e.dma_start(
                               out=x_f32[dst0:dst0 + n, slot * DM:(slot + 1) * DM],
                               in_=x_in[seq * SEQ + src0: seq * SEQ + src0 + n, :]),
                           writes=[f"xf{slot}"])

        def stage_fm(blk):
            buf = blk % 2
            for s in range(4):
                sg = blk * 4 + s
                slot = sg % 8
                cb = sg % 2
                sc.op("vector", lambda e, slot=slot, cb=cb: e.tensor_copy(
                    xtok_bf[:, cb * DM:(cb + 1) * DM],
                    x_f32[:, slot * DM:(slot + 1) * DM]),
                    reads=[f"xf{slot}"], writes=[f"xtok{cb}"])
                for dc in range(8):
                    sc.op("tensor", lambda e, dc=dc, cb=cb: e.transpose(
                        TX[:, dc * 128:(dc + 1) * 128],
                        xtok_bf[:, cb * DM + dc * 128: cb * DM + (dc + 1) * 128],
                        idb_sb[:]),
                        reads=[f"xtok{cb}", "idb"],
                        writes=["TX"] if dc in (0, 7) else [])
                sc.op("vector", lambda e, s=s, buf=buf: e.tensor_copy(
                    x_fm[:].rearrange("p (b dc t) -> p b dc t", b=2, dc=8)[:, buf, :, s * 128:(s + 1) * 128],
                    TX[:].rearrange("p (dc t) -> p dc t", dc=8)),
                    reads=["TX"], writes=[f"xfm{buf}_{s}"])

        stage_load(0)
        stage_fm(0)
        for blk in range(NBLK):
            buf = blk % 2
            seq = blk // 4
            xfm_keys = [f"xfm{buf}_{s}" for s in range(4)]

            if L1_STAGE < 2:
                continue
            for fc in range(16):
                half = fc % 2
                is_q = fc < 8
                c = fc if is_q else fc - 8
                for dc in range(8):
                    sc.op("tensor", lambda e, fc=fc, dc=dc, half=half, buf=buf: e.matmul(
                        PB12[:, half * 512:(half + 1) * 512],
                        wqkv_bf[:, dc * 3072 + fc * 128: dc * 3072 + (fc + 1) * 128],
                        x_fm[:, (buf * 8 + dc) * 512:(buf * 8 + dc + 1) * 512],
                        start=(dc == 0), stop=(dc == 7)),
                        reads=[f"wq{dc}"] + (xfm_keys if dc in (0, 7) else []),
                        writes=[f"PB12{half}"] if dc in (0, 7) else [])
                bias_ap = bq8_sb[:, c:c + 1] if is_q else bqkv_sb[:, 8 + c:9 + c]
                dst = q_fm if is_q else k_fm
                sc.op("scalar", lambda e, dst=dst, c=c, half=half, bias_ap=bias_ap,
                      scl=(0.125 if is_q else 1.0):
                      e.activation(dst[:, c * 512:(c + 1) * 512],
                                   PB12[:, half * 512:(half + 1) * 512], AF.Identity,
                                   bias=bias_ap, scale=scl),
                      reads=[f"PB12{half}", "bq8" if is_q else f"bq{8 + c}"],
                      writes=[("qfm" if is_q else "kfm") + str(c)])

            if L1_STAGE < 3:
                continue
            for s in range(4):
                for half in range(2):
                    sc.op("tensor", lambda e, half=half: e.matmul(
                        PB34[:, half * 512:(half + 1) * 512], onerb_sb[:],
                        bv_row[:, half * 512:(half + 1) * 512],
                        start=True, stop=False, skip_group_check=True),
                        reads=["onerb", "bv"], writes=[f"PB34{half}"])
                    for dc in range(8):
                        sc.op("tensor", lambda e, s=s, half=half, dc=dc, buf=buf: e.matmul(
                            PB34[:, half * 512:(half + 1) * 512],
                            x_fm[:, (buf * 8 + dc) * 512 + s * 128:(buf * 8 + dc) * 512 + (s + 1) * 128],
                            wqkv_bf[:, dc * 3072 + 2048 + half * 512: dc * 3072 + 2048 + (half + 1) * 512],
                            start=False, stop=(dc == 7), skip_group_check=True),
                            reads=[f"wq{dc}"] + (xfm_keys if dc in (0, 7) else []),
                            writes=[f"PB34{half}"] if dc == 7 else [])
                    sc.op("scalar", lambda e, s=s, half=half: e.activation(
                        v_bf[:, s * DM + half * 512: s * DM + (half + 1) * 512],
                        PB34[:, half * 512:(half + 1) * 512], AF.Identity,
                        bias=0.0, scale=1.0),
                        reads=[f"PB34{half}"], writes=[f"vbf{s}_{half}"])

            if blk + 1 < NBLK:
                stage_load(blk + 1)
                stage_fm(blk + 1)

            if L1_STAGE < 4:
                continue

            def emit_S(s):
                pp = s % 3
                for h in range(H):
                    hp = h // 2
                    for qg in range(4):
                        ps = PB5 if h % 2 == 0 else PB6
                        wkey = "PS0" if h % 2 == 0 else "PS1"
                        first = (qg == 0 and h < 2)
                        last = (qg == 3 and h >= 14)
                        sc.op("tensor", lambda e, h=h, hp=hp, qg=qg, s=s, ps=ps: e.matmul(
                            ps[qg * 32:(qg + 1) * 32, hp * 32:(hp + 1) * 32],
                            q_fm[(h % 2) * 64:(h % 2) * 64 + 64,
                                 hp * 512 + s * 128 + qg * 32: hp * 512 + s * 128 + (qg + 1) * 32],
                            k_fm[(h % 2) * 64:(h % 2) * 64 + 64,
                                 hp * 512 + s * 128 + qg * 32: hp * 512 + s * 128 + (qg + 1) * 32],
                            start=True, stop=True,
                            tile_position=((h % 2) * 64, qg * 32)),
                            reads=[f"qfm{hp}", f"kfm{hp}"],
                            writes=[wkey] if (first or last) else [])

            def emit_softmax(s):
                pp = s % 3
                sc.op("vector", lambda e, pp=pp: e.tensor_tensor(
                    S_sb[:].rearrange("p (b hp two g) -> p b hp two g", b=3, hp=8, two=2)[:, pp, :, 0, :],
                    PB5[:, 0:256].rearrange("p (hp g) -> p hp g", hp=8),
                    mb0_sb[:].rearrange("p (hp g) -> p hp g", hp=8), ALU.add),
                    reads=["PS0", "mb0"], writes=[f"S{pp}a"])
                sc.op("vector", lambda e, pp=pp: e.tensor_tensor(
                    S_sb[:].rearrange("p (b hp two g) -> p b hp two g", b=3, hp=8, two=2)[:, pp, :, 1, :],
                    PB6[:, 0:256].rearrange("p (hp g) -> p hp g", hp=8),
                    mb1_sb[:].rearrange("p (hp g) -> p hp g", hp=8), ALU.add),
                    reads=["PS1", "mb1"], writes=[f"S{pp}b"])
                sc.op("vector", lambda e, pp=pp: e.reduce_max(
                    negmx[:, pp:pp + 1], S_sb[:, pp * 512:(pp + 1) * 512], AX.X, negate=True),
                    reads=[f"S{pp}a", f"S{pp}b"], writes=[f"nm{pp}"])
                sc.op("scalar", lambda e, pp=pp: e.activation(
                    P_bf[:, pp * 512:(pp + 1) * 512], S_sb[:, pp * 512:(pp + 1) * 512],
                    AF.Exp, bias=negmx[:, pp:pp + 1], scale=1.0),
                    reads=[f"S{pp}a", f"S{pp}b", f"nm{pp}"], writes=[f"P{pp}"])
                sc.op("vector", lambda e, pp=pp: e.reduce_sum(
                    dn_sb[:, pp * 16:(pp + 1) * 16].rearrange("p (h o) -> p h o", o=1),
                    P_bf[:, pp * 512:(pp + 1) * 512].rearrange("p (h k) -> p h k", h=16), AX.X),
                    reads=[f"P{pp}"], writes=[f"dn{pp}"])
                sc.op("vector", lambda e, pp=pp: e.reciprocal(
                    r_sb[:, pp * 16:(pp + 1) * 16], dn_sb[:, pp * 16:(pp + 1) * 16]),
                    reads=[f"dn{pp}"], writes=[f"r{pp}"])
                sc.op("vector", lambda e, pp=pp: e.tensor_tensor(
                    P_bf[:, pp * 512:(pp + 1) * 512].rearrange("p (h k) -> p h k", h=16),
                    P_bf[:, pp * 512:(pp + 1) * 512].rearrange("p (h k) -> p h k", h=16),
                    r_sb[:, pp * 16:(pp + 1) * 16].unsqueeze(2).broadcast_to((128, 16, 32)),
                    ALU.mult),
                    reads=[f"P{pp}", f"r{pp}"], writes=[f"P{pp}"])
                sc.op("vector", lambda e, pp=pp: e.transpose(
                    Pt_bf[:, pp * 512:(pp + 1) * 512], P_bf[:, pp * 512:(pp + 1) * 512]),
                    reads=[f"P{pp}"], writes=[f"Pt{pp}"])

            def emit_PV(s):
                pp = s % 3
                pvkeys = ["PB120", "PB121"]
                for hv in range(2):
                    for h in range(H):
                        hp = h // 2
                        for qgl in range(2):
                            qg = hv * 2 + qgl
                            first = (h == 0 and qgl == 0)
                            last = (h == 15 and qgl == 1)
                            off = qgl * 512 + hp * 64 + hv * 32
                            sc.op("tensor", lambda e, h=h, qg=qg, s=s, pp=pp, off=off:
                                  e.matmul(
                                      PB12[(h % 2) * 64:(h % 2) * 64 + 64, off:off + 32],
                                      v_bf[qg * 32:(qg + 1) * 32,
                                           s * DM + h * 64: s * DM + (h + 1) * 64],
                                      Pt_bf[qg * 32:(qg + 1) * 32,
                                            pp * 512 + h * 32: pp * 512 + (h + 1) * 32],
                                      start=True, stop=True,
                                      tile_position=(qg * 32, (h % 2) * 64)),
                                  reads=[f"Pt{pp}", f"vbf{s}_0", f"vbf{s}_1"],
                                  writes=(pvkeys if (first or last) else []))
                for qgl in range(2):
                    sc.op("scalar", lambda e, s=s, qgl=qgl: e.activation(
                        attn_fm[:].rearrange("p (hp st qg2 qgl qi) -> p st qgl hp qg2 qi",
                                             hp=8, st=4, qg2=2, qgl=2)[:, s, qgl, :, :, :],
                        PB12[:, qgl * 512:(qgl + 1) * 512]
                        .rearrange("p (hp hv qi) -> p hp hv qi", hp=8, hv=2),
                        AF.Identity, bias=0.0, scale=1.0),
                        reads=pvkeys, writes=[f"atf{s}_{qgl}"])

            def emit_proj(s):
                sg = blk * 4 + s
                slot = sg % 8
                cb = sg % 2
                lb = 8 * (s % 2)
                lp = s % 2
                seq_l = blk // 4
                atf_keys = [f"atf{s}_0", f"atf{s}_1"]
                sc.op("tensor", lambda e: e.matmul(
                    PB34[:, 0:512], onerb_sb[:], bproj_row[:, 0:512],
                    start=True, stop=False, skip_group_check=True),
                    reads=["onerb", "bp"] + atf_keys, writes=["PB340"])
                sc.op("tensor", lambda e: e.matmul(
                    PB34[:, 512:1024], onerb_sb[:], bproj_row[:, 512:1024],
                    start=True, stop=False, skip_group_check=True),
                    reads=["onerb", "bp"], writes=["PB341"])
                for c in range(8):
                    for half in range(2):
                        sc.op("tensor", lambda e, c=c, half=half, s=s: e.matmul(
                            PB34[:, half * 512:(half + 1) * 512],
                            attn_fm[:, c * 512 + s * 128: c * 512 + (s + 1) * 128],
                            wproj_bf[:, c * 1024 + half * 512: c * 1024 + (half + 1) * 512],
                            start=False, stop=(c == 7), skip_group_check=True),
                            reads=[f"wp{c}"] + atf_keys,
                            writes=[f"PB34{half}"] if c == 7 else [])
                for half in range(2):
                    sc.op("vector", lambda e, slot=slot, cb=cb, half=half: e.tensor_tensor(
                        y_f32[:, cb * DM + half * 512: cb * DM + (half + 1) * 512],
                        PB34[:, half * 512:(half + 1) * 512],
                        x_f32[:, slot * DM + half * 512: slot * DM + (half + 1) * 512],
                        ALU.add),
                        reads=[f"PB34{half}", f"xf{slot}"],
                        writes=[f"y{cb}_{half}"])
                ykeys = [f"y{cb}_0", f"y{cb}_1"]
                sc.op("scalar", lambda e, cb=cb, lb=lb: e.activation(
                    x1f_sb[:, cb * DM:(cb + 1) * DM], y_f32[:, cb * DM:(cb + 1) * DM],
                    AF.Identity, bias=0.0, scale=1.0,
                    accum_out=ln_sm[:, lb + 0:lb + 1]),
                    reads=ykeys, writes=[f"lnsum{lp}", f"x1f{cb}"])
                sc.op("scalar", lambda e, cb=cb, lb=lb: e.activation(
                    x1f_sb[:, cb * DM:(cb + 1) * DM], y_f32[:, cb * DM:(cb + 1) * DM], AF.Square,
                    accum_out=ln_sm[:, lb + 1:lb + 2]),
                    reads=ykeys, writes=[f"lnsq{lp}", f"x1f{cb}"])
                sc.op("vector", lambda e, lb=lb: e.tensor_scalar_mul(
                    ln_sm[:, lb + 2:lb + 3], ln_sm[:, lb + 0:lb + 1], 1.0 / DM),
                    reads=[f"lnsum{lp}"], writes=[f"lnm{lp}"])
                sc.op("vector", lambda e, lb=lb: e.tensor_tensor(
                    ln_sm[:, lb + 3:lb + 4], ln_sm[:, lb + 2:lb + 3], ln_sm[:, lb + 2:lb + 3],
                    ALU.mult),
                    reads=[f"lnm{lp}"], writes=[f"lnm2{lp}"])
                sc.op("vector", lambda e, lb=lb: e.scalar_tensor_tensor(
                    ln_sm[:, lb + 4:lb + 5], ln_sm[:, lb + 1:lb + 2], 1.0 / DM,
                    ln_sm[:, lb + 3:lb + 4], ALU.mult, ALU.subtract),
                    reads=[f"lnsq{lp}", f"lnm2{lp}"], writes=[f"lnvar{lp}"])
                sc.op("scalar", lambda e, lb=lb: e.activation(
                    ln_sm[:, lb + 5:lb + 6], ln_sm[:, lb + 4:lb + 5], AF.Ln, bias=eps_sb[:], scale=1.0),
                    reads=[f"lnvar{lp}", "eps"], writes=[f"lnln{lp}"])
                sc.op("scalar", lambda e, lb=lb: e.activation(
                    ln_sm[:, lb + 6:lb + 7], ln_sm[:, lb + 5:lb + 6], AF.Exp, bias=0.0, scale=-0.5),
                    reads=[f"lnln{lp}"], writes=[f"lnrs{lp}"])
                sc.op("vector", lambda e, lb=lb: e.scalar_tensor_tensor(
                    ln_sm[:, lb + 7:lb + 8], ln_sm[:, lb + 2:lb + 3], -1.0,
                    ln_sm[:, lb + 6:lb + 7], ALU.mult, ALU.mult),
                    reads=[f"lnm{lp}", f"lnrs{lp}"], writes=[f"lnmb{lp}"])
                sc.op("scalar", lambda e, cb=cb, lb=lb: e.activation(
                    x1f_sb[:, cb * DM:(cb + 1) * DM], y_f32[:, cb * DM:(cb + 1) * DM], AF.Identity,
                    bias=ln_sm[:, lb + 7:lb + 8], scale=ln_sm[:, lb + 6:lb + 7]),
                    reads=ykeys + [f"lnmb{lp}", f"lnrs{lp}"], writes=[f"x1f{cb}"])
                sc.op("vector", lambda e, cb=cb: e.tensor_copy(
                    x1b_sb[:, cb * DM:(cb + 1) * DM], x1f_sb[:, cb * DM:(cb + 1) * DM]),
                    reads=[f"x1f{cb}"], writes=[f"x1b{cb}"])
                j = sg % 16
                for dst0, src0, n in _rolled_pieces(j):
                    sc.dma("x1f", lambda e, cb=cb, dst0=dst0, src0=src0, n=n, seq=seq_l:
                           e.dma_start(
                               out=x1f_out[seq * SEQ + src0: seq * SEQ + src0 + n, :],
                               in_=x1f_sb[dst0:dst0 + n, cb * DM:(cb + 1) * DM]),
                           reads=[f"x1f{cb}"], writes=[f"x1fo{sg}"])
                    sc.dma("x1b", lambda e, cb=cb, dst0=dst0, src0=src0, n=n, seq=seq_l:
                           e.dma_start(
                               out=x1b_out[seq * SEQ + src0: seq * SEQ + src0 + n, :],
                               in_=x1b_sb[dst0:dst0 + n, cb * DM:(cb + 1) * DM]),
                           reads=[f"x1b{cb}"], writes=[f"x1bo{sg}"])

            emit_S(0); emit_softmax(0)
            emit_S(1); emit_softmax(1)
            emit_S(2); emit_softmax(2)
            emit_PV(0)
            emit_S(3); emit_softmax(3)
            emit_PV(1)
            emit_proj(0)
            emit_PV(2)
            emit_proj(1)
            emit_PV(3)
            emit_proj(2)
            emit_proj(3)

        skip_a2 = L1_PHASES == "A"
        all_x1b = [f"x1bo{sg}" for sg in range(NSUB)]
        for rb in range(0 if skip_a2 else NBLK):
            for dc in range(8):
                sc.dma("xl", lambda e, rb=rb, dc=dc: e.dma_start_transpose(
                    x1_fm[:, dc * 512:(dc + 1) * 512],
                    x1b_out[rb * 512:(rb + 1) * 512, dc * 128:(dc + 1) * 128]),
                    reads=(all_x1b if (rb == 0 and dc == 0) else []),
                    writes=[f"x1fmc{dc}"] + ([f"qfm{c}" for c in range(8)]
                                             if (rb == 0 and dc == 0) else []))
            x1fm_keys = [f"x1fmc{dc}" for dc in range(8)]
            for dc in range(8):
                sc.op("tensor", lambda e, dc=dc: e.matmul(
                    PB7[0:8, 0:512],
                    wr_bf[:, dc * 8:(dc + 1) * 8],
                    x1_fm[:, dc * 512:(dc + 1) * 512],
                    start=(dc == 0), stop=(dc == 7)),
                    reads=[f"wr{dc}", f"x1fmc{dc}"],
                    writes=["PB7"] if dc in (0, 7) else [])
            sc.op("vector", lambda e: e.tensor_copy(lg_sb[:], PB7[0:8, 0:512]),
                  reads=["PB7"],
                  writes=["lg"] + ([f"kfm{c}" for c in range(8)] if rb == 0 else []))
            for s in range(4):
                st_i = rb * 4 + s
                pp = s % 2
                sc.op("tensor", lambda e, s=s: e.transpose(
                    PB5[:, 0:8], lg_sb[0:8, s * 128:(s + 1) * 128], idf_sb[0:8, 0:8]),
                    reads=["lg", "idf"], writes=["PS0"])
                sc.op("vector", lambda e, pp=pp: e.tensor_copy(
                    ltok_sb[:, pp * 8:(pp + 1) * 8], PB5[:, 0:8]),
                    reads=["PS0"], writes=[f"lt{pp}"])
                sc.op("vector", lambda e, pp=pp: e.reduce_max(
                    nm8_sb[:, pp:pp + 1], ltok_sb[:, pp * 8:(pp + 1) * 8], AX.X, negate=True),
                    reads=[f"lt{pp}"], writes=[f"nm8{pp}"])
                sc.op("scalar", lambda e, pp=pp: e.activation(
                    e8_sb[:, pp * 8:(pp + 1) * 8], ltok_sb[:, pp * 8:(pp + 1) * 8],
                    AF.Exp, bias=nm8_sb[:, pp:pp + 1], scale=1.0,
                    accum_out=s8_sb[:, pp:pp + 1]),
                    reads=[f"lt{pp}", f"nm8{pp}"], writes=[f"s8{pp}", f"e8{pp}"]),
                sc.op("vector", lambda e, st_i=st_i, pp=pp: e.reciprocal(
                    p_all[:, st_i:st_i + 1], s8_sb[:, pp:pp + 1]),
                    reads=[f"s8{pp}"], writes=["p_all"])
                sc.op("vector", lambda e, pp=pp: e.max(
                    mx8_sb[:, pp * 8:(pp + 1) * 8], ltok_sb[:, pp * 8:(pp + 1) * 8]),
                    reads=[f"lt{pp}"], writes=[f"mx8{pp}"])
                sc.op("vector", lambda e, pp=pp: e.max_index(
                    mi8_sb[:, pp * 8:(pp + 1) * 8], mx8_sb[:, pp * 8:(pp + 1) * 8],
                    ltok_sb[:, pp * 8:(pp + 1) * 8]),
                    reads=[f"mx8{pp}", f"lt{pp}"], writes=[f"mi8{pp}"])
                sc.op("vector", lambda e, st_i=st_i, pp=pp: e.tensor_copy(
                    eidx_all[:, st_i:st_i + 1], mi8_sb[:, pp * 8:pp * 8 + 1]),
                    reads=[f"mi8{pp}"], writes=["eidx_all"])
                sc.op("vector", lambda e, st_i=st_i: e.tensor_tensor(
                    O_all[:, st_i * 8:(st_i + 1) * 8], iota8_sb[:],
                    eidx_all[:, st_i:st_i + 1].broadcast_to((128, 8)), ALU.is_equal),
                    reads=["eidx_all", "iota8"], writes=["O_all"])

        if skip_a2:
            fin = []
            for stn in ("x1f", "x1b"):
                fin += sc.total(stn)
            fin += [("eng:scalar", sc.cnt["eng:scalar"]), ("eng:gpsimd", sc.cnt["eng:gpsimd"]),
                    ("eng:vector", sc.cnt["eng:vector"]), ("eng:tensor", sc.cnt["eng:tensor"])]
            sc.op("sync", lambda e: e.nop(), extra=fin)
            sc.emit(block)
            return nc
        sc.op("tensor", lambda e: e.matmul(
            PB12[:, 0:512], l128_sb[:], O_all[:], start=True, stop=False,
            skip_group_check=True),
            reads=["l128", "O_all"], writes=["PB120", "PB121"])
        sc.op("tensor", lambda e: e.matmul(
            PB7[0:1, 0:512], onec_sb[:], O_all[:], start=True, stop=True,
            skip_group_check=True),
            reads=["onec", "O_all"], writes=["PB7"])
        sc.op("vector", lambda e: e.tensor_copy(cnt_row[:], PB7[0:1, 0:512]),
              reads=["PB7"], writes=["cntrow"])
        sc.dma("bnc", lambda e: e.dma_start(
            out=cnt_dram.ap().rearrange("a b -> (a b)")[None, :], in_=cnt_row[:]),
            reads=["cntrow"], writes=["cntd"])
        sc.dma("bnc", lambda e: e.dma_start(out=cnt64[:], in_=cnt_dram.ap()),
               reads=["cntd"], writes=["cnt64"])
        sc.op("tensor", lambda e: e.matmul(
            PB6[0:64, 0:8], l128_sb[0:64, 0:64], cnt64[:], start=True, stop=True,
            skip_group_check=True),
            reads=["l128", "cnt64"], writes=["PS1"])
        sc.op("vector", lambda e: e.tensor_copy(to64[:], PB6[0:64, 0:8]),
              reads=["PS1"], writes=["to64"])
        sc.op("tensor", lambda e: e.matmul(
            PB6[0:1, 8:16], onec_sb[0:64, :], cnt64[:], start=True, stop=True,
            skip_group_check=True),
            reads=["onec", "cnt64"], writes=["PS1"])
        sc.op("vector", lambda e: e.tensor_copy(tot_row[:], PB6[0:1, 8:16]),
              reads=["PS1"], writes=["totrow"])
        t_tot = sc.dma("bnc", lambda e: e.dma_start(out=tot_out, in_=tot_row[:]),
                       reads=["totrow"], writes=["toto"])
        sc.dma("bnc", lambda e: e.dma_start(out=to_dram.ap(), in_=to64[:]),
               reads=["to64"], writes=["tod"])
        sc.dma("bnc", lambda e: e.dma_start(
            out=to_row[:], in_=to_dram.ap().rearrange("a b -> (a b)")[None, :]),
            reads=["tod"], writes=["torow"])
        sc.op("tensor", lambda e: e.matmul(
            PB12[:, 0:512], onerf_sb[:], to_row[:], start=False, stop=True,
            skip_group_check=True),
            reads=["onerf", "torow", "PB120", "PB121"], writes=["PB120", "PB121"])
        sc.op("vector", lambda e: e.tensor_copy(RL_all, PB12[:, 0:512]),
              reads=["PB120", "PB121"], writes=["RL", "P0", "P1"])
        sc.op("vector", lambda e: e.tensor_tensor(trl_sb, RL_all, O_all[:], ALU.mult),
              reads=["RL", "O_all"], writes=["trl", "S0a", "S0b", "S1a", "S1b"])
        sc.op("vector", lambda e: e.reduce_sum(
            rankl_all[:].rearrange("p (t o) -> p t o", o=1),
            trl_sb.rearrange("p (t e) -> p t e", e=8), AX.X),
            reads=["trl"], writes=["rankl"])
        t_o1 = sc.dma("bnc", lambda e: e.dma_start(out=rankl_out, in_=rankl_all[:]),
                      reads=["rankl"], writes=["o1"])
        t_o2 = sc.dma("bnc", lambda e: e.dma_start(out=eidx_out, in_=eidx_all[:]),
                      reads=["eidx_all"], writes=["o2"])
        t_o3 = sc.dma("bnc", lambda e: e.dma_start(out=p_out, in_=p_all[:]),
                      reads=["p_all"], writes=["o3"])

        fin = [t_o1, t_o2, t_o3, t_tot]
        for stn in ("x1f", "x1b", "bnc"):
            fin += sc.total(stn)
        fin += [("eng:scalar", sc.cnt["eng:scalar"]), ("eng:gpsimd", sc.cnt["eng:gpsimd"]),
                ("eng:vector", sc.cnt["eng:vector"]), ("eng:tensor", sc.cnt["eng:tensor"])]
        sc.op("sync", lambda e: e.nop(), extra=fin)

        sc.emit(block)
    return nc



def build_l2(nt):
    nc = bass.Bass("TRN2", target_bir_lowering=False, detect_race_conditions=False)

    x1f_in = nc.dram_tensor("x1f", [T, DM], F32, kind="ExternalInput").ap()
    x1b_in = nc.dram_tensor("x1b", [T, DM], BF16, kind="ExternalInput").ap()
    eidx_in = nc.dram_tensor("eidx", [128, NSUB], F32, kind="ExternalInput").ap()
    p_in = nc.dram_tensor("ptok", [128, NSUB], F32, kind="ExternalInput").ap()
    rankl_in = nc.dram_tensor("rankl", [128, NSUB], F32, kind="ExternalInput").ap()
    cnts_in = nc.dram_tensor("cnts", [NE, NE], F32, kind="ExternalInput").ap()
    cmask_in = nc.dram_tensor("cmask", [NE, 1], F32, kind="ExternalInput").ap()
    w1_in = nc.dram_tensor("w1", [NE, DM, 4096], F32, kind="ExternalInput").ap()
    w2_in = nc.dram_tensor("w2", [NE, 4096, DM], F32, kind="ExternalInput").ap()

    out_o = nc.dram_tensor("out", [T, DM], F32, kind="ExternalOutput").ap()

    offt = [0]
    for e in range(NE):
        offt.append(offt[-1] + nt[e] * 128)
    TOT = offt[NE]

    gidx_dram = nc.dram_tensor("gidx_scr", [TOT + T, 1], I32, kind="Internal")
    ye_buf = nc.dram_tensor("ye_buf", [TOT + 1, DM], BF16, kind="Internal")
    cnt2_dram = nc.dram_tensor("cnt2_scr", [NSUB, NE], F32, kind="Internal")
    to2_dram = nc.dram_tensor("to2_scr", [NSUB, NE], F32, kind="Internal")

    ident_bf_c = nc.inline_tensor(np.eye(128, dtype=ml_dtypes.bfloat16), "ident_bf")
    onesrf_c = nc.inline_tensor(np.ones((1, 128), dtype=np.float32), "onesrf")
    onesc_c = nc.inline_tensor(np.ones((128, 1), dtype=np.float32), "onesc")
    l128_c = nc.inline_tensor(np.triu(np.ones((128, 128), dtype=np.float32), 1), "l128")
    iota8_c = nc.inline_tensor(
        np.tile(np.arange(8, dtype=np.float32)[None, :], (128, 1)), "iota8")
    trash_c = nc.inline_tensor(np.full((128, NSUB), float(TOT), dtype=np.float32), "trash")
    ttok_c = nc.inline_tensor(
        (float(TOT) + np.arange(NSUB, dtype=np.float32)[None, :] * 128
         + np.arange(128, dtype=np.float32)[:, None]).astype(np.float32), "ttok")
    tokid_c = nc.inline_tensor(
        (np.arange(NSUB, dtype=np.int32)[None, :] * 128
         + np.arange(128, dtype=np.int32)[:, None]).astype(np.int32), "tokid")
    offr_c = nc.inline_tensor(np.array([[float(v) for v in offt[:NE]]], dtype=np.float32), "offr")

    with ExitStack() as st:
        en = st.enter_context
        sc = Sched(nc, st)
        sc.stream("cst", "sync", 2)
        sc.stream("xsc", "sync", 4)
        sc.stream("xet", "gpsimd", 4)
        sc.stream("zzf", "sync", 2)
        sc.stream("x1d", "sync", 4)
        sc.stream("zz", "gpsimd", 4)
        sc.stream("sct", "gpsimd", 16)
        sc.stream("w1", "gpsimd", 4)
        sc.stream("w2", "gpsimd", 4)
        sc.stream("yg", "gpsimd", 4)
        sc.stream("yeo", "scalar", 4)
        sc.stream("out", "scalar", 4)
        sc.stream("bnc", "sync", 2)

        idb_sb = en(nc.sbuf_tensor("idb_sb", [128, 128], BF16))
        onerf_sb = en(nc.sbuf_tensor("onerf_sb", [1, 128], F32))
        onec_sb = en(nc.sbuf_tensor("onec_sb", [128, 1], F32))
        l128_sb = en(nc.sbuf_tensor("l128_sb", [128, 128], F32))
        iotat_sb = en(nc.sbuf_tensor("iotat_sb", [128, 8], F32))
        trash_sb = en(nc.sbuf_tensor("trash_sb", [128, NSUB], F32))
        offr_sb = en(nc.sbuf_tensor("offr_sb", [1, NE], F32))
        eidx_sb = en(nc.sbuf_tensor("eidx_sb", [128, NSUB], F32))
        p_sb = en(nc.sbuf_tensor("p_sb", [128, NSUB], F32))
        rankl_sb = en(nc.sbuf_tensor("rankl_sb", [128, NSUB], F32))
        cnts_sb = en(nc.sbuf_tensor("cnts_sb", [NE, NE], F32))
        cmask_sb = en(nc.sbuf_tensor("cmask_sb", [NE, 1], F32))
        co_row = en(nc.sbuf_tensor("co_row", [1, NE], F32))
        co_bc = en(nc.sbuf_tensor("co_bc", [128, NE], F32))
        off_bc = en(nc.sbuf_tensor("off_bc", [128, NE], F32))
        co_own = en(nc.sbuf_tensor("co_own", [128, NSUB], F32))
        rankg_sb = en(nc.sbuf_tensor("rankg_sb", [128, NSUB], F32))
        keep_sb = en(nc.sbuf_tensor("keep_sb", [128, NSUB], F32))
        s_all = en(nc.sbuf_tensor("s_all", [128, NSUB], F32))
        nk_i = en(nc.sbuf_tensor("nk_i", [128, NSUB], I32))
        krank_sb = en(nc.sbuf_tensor("krank_sb", [128, NSUB], F32))
        offown_sb = en(nc.sbuf_tensor("offown_sb", [128, NSUB], F32))
        slotf_sb = en(nc.sbuf_tensor("slotf_sb", [128, NSUB], F32))
        slot_i = en(nc.sbuf_tensor("slot_i", [128, NSUB], I32))
        tokid_sb = en(nc.sbuf_tensor("tokid_sb", [128, NSUB], I32))
        ttok_sb = en(nc.sbuf_tensor("ttok_sb", [128, NSUB], F32))
        slotd_f = en(nc.sbuf_tensor("slotd_f", [128, NSUB], F32))
        slotd_i = en(nc.sbuf_tensor("slotd_i", [128, NSUB], I32))
        gidx_sb = en(nc.sbuf_tensor("gidx_sb", [128, (TOT // 128) + 1], I32))
        cnt64_2 = en(nc.sbuf_tensor("cnt64_2", [NSUB, 8], F32))
        to64_2 = en(nc.sbuf_tensor("to64_2", [NSUB, 8], F32))

        xet_bf = en(nc.sbuf_tensor("xet_bf", [128, 4 * DM], BF16))
        xe_fm = en(nc.sbuf_tensor("xe_fm", [128, 8 * 1024], BF16))
        w1g_bf = en(nc.sbuf_tensor("w1g_bf", [128, 2 * 4096], BF16))
        w2_sb = en(nc.sbuf_tensor("w2_sb", [128, 32 * 1024], BF16))
        h_sb = en(nc.sbuf_tensor("h_sb", [128, 32 * 1024], BF16))
        yet_bf = en(nc.sbuf_tensor("yet_bf", [128, 2 * DM], BF16))
        yg_bf = en(nc.sbuf_tensor("yg_bf", [128, 3 * DM], BF16))
        x1d_f = en(nc.sbuf_tensor("x1d_f", [128, 3 * DM], F32))
        ln_sm = en(nc.sbuf_tensor("ln_sm", [128, 24], F32))
        eps_sb = en(nc.sbuf_tensor("eps_sb", [128, 1], F32))
        O_all = h_sb[:, 1024:2048].bitcast(F32)
        O_kept = h_sb[:, 2048:3072].bitcast(F32)
        t1_sb = h_sb[:, 3072:4096].bitcast(F32)
        zrow_bf = h_sb[:, 4096:5120]
        RK_all = h_sb[:, 5120:6144].bitcast(F32)
        sq_scr = h_sb[:, 6144:8192].bitcast(F32)
        cnt_row2 = h_sb[0:1, 8192:9216].bitcast(F32)
        to_row2 = h_sb[0:1, 9216:10240].bitcast(F32)
        y2_f = h_sb[:, 12288:18432].bitcast(F32)
        outt_f = x1d_f
        ALIAS_KEYS = ["O_all", "O_kept", "t1", "zrow", "RK", "sq_alias",
                      "cntrow2", "torow2"]

        TXC = en(nc.psum_tensor("TXC", [128, 1024], BF16))
        PH0 = en(nc.psum_tensor("PH0", [128, 512], F32))
        PH1 = en(nc.psum_tensor("PH1", [128, 512], F32))
        PY0 = en(nc.psum_tensor("PY0", [128, 1024], F32))
        PY1 = en(nc.psum_tensor("PY1", [128, 1024], F32))
        PSM = en(nc.psum_tensor("PSM", [128, 512], F32))

        block = en(nc.Block())

        for dst, src, key in ((idb_sb, ident_bf_c.ap(), "idb"),
                              (onerf_sb, onesrf_c.ap(), "onerf"),
                              (onec_sb, onesc_c.ap(), "onec"),
                              (l128_sb, l128_c.ap(), "l128"),
                              (iotat_sb, iota8_c.ap(), "iotat"),
                              (trash_sb, trash_c.ap(), "trash"),
                              (offr_sb, offr_c.ap(), "offr"),
                              (eidx_sb, eidx_in, "eidx"), (p_sb, p_in, "p"),
                              (rankl_sb, rankl_in, "rankl"), (cnts_sb, cnts_in, "cnts"),
                              (cmask_sb, cmask_in, "cmask"), (tokid_sb, tokid_c.ap(), "tokid"),
                              (ttok_sb, ttok_c.ap(), "ttok")):
            sc.dma("cst", lambda e, d=dst, s=src: e.dma_start(out=d[:], in_=s),
                   writes=[key])
        sc.op("vector", lambda e: e.memset(eps_sb[:], EPS), writes=["eps"])

        sc.op("vector", lambda e: e.tensor_tensor(
            O_all.rearrange("p (t e) -> p t e", e=8),
            iotat_sb[:].unsqueeze(1).broadcast_to((128, NSUB, 8)),
            eidx_sb[:].unsqueeze(2).broadcast_to((128, NSUB, 8)), ALU.is_equal),
            reads=["iotat", "eidx"], writes=["O_all"])
        sc.op("tensor", lambda e: e.matmul(
            PSM[0:1, 0:8], cmask_sb[:], cnts_sb[:], start=True, stop=True,
            skip_group_check=True),
            reads=["cmask", "cnts"], writes=["PSM"])
        sc.op("vector", lambda e: e.tensor_copy(co_row[:], PSM[0:1, 0:8]),
              reads=["PSM"], writes=["corow"])
        sc.op("tensor", lambda e: e.matmul(
            PSM[:, 8:16], onerf_sb[:], co_row[:], start=True, stop=True,
            skip_group_check=True),
            reads=["onerf", "corow", "PSM"], writes=["PSM"])
        sc.op("vector", lambda e: e.tensor_copy(co_bc[:], PSM[:, 8:16]),
              reads=["PSM"], writes=["cobc"])
        sc.op("tensor", lambda e: e.matmul(
            PSM[:, 16:24], onerf_sb[:], offr_sb[:], start=True, stop=True,
            skip_group_check=True),
            reads=["onerf", "offr", "PSM"], writes=["PSM"])
        sc.op("vector", lambda e: e.tensor_copy(off_bc[:], PSM[:, 16:24]),
              reads=["PSM"], writes=["offbc"])
        sc.op("vector", lambda e: e.tensor_tensor(
            t1_sb.rearrange("p (t e) -> p t e", e=8),
            O_all.rearrange("p (t e) -> p t e", e=8),
            co_bc[:].unsqueeze(1).broadcast_to((128, NSUB, 8)), ALU.mult),
            reads=["O_all", "cobc"], writes=["t1"])
        sc.op("vector", lambda e: e.reduce_sum(
            co_own[:].rearrange("p (t o) -> p t o", o=1),
            t1_sb.rearrange("p (t e) -> p t e", e=8), AX.X),
            reads=["t1"], writes=["coown"])
        sc.op("vector", lambda e: e.tensor_tensor(rankg_sb[:], rankl_sb[:], co_own[:], ALU.add),
              reads=["rankl", "coown"], writes=["rankg"])
        sc.op("vector", lambda e: e.tensor_single_scalar(keep_sb[:], rankg_sb[:], float(T), ALU.is_lt),
              reads=["rankg"], writes=["keep"])
        sc.op("vector", lambda e: e.tensor_single_scalar(nk_i[:], rankg_sb[:], float(T), ALU.is_ge),
              reads=["rankg"], writes=["nk"])
        sc.op("vector", lambda e: e.tensor_tensor(s_all[:], keep_sb[:], p_sb[:], ALU.mult),
              reads=["keep", "p"], writes=["sall"])
        sc.op("vector", lambda e: e.tensor_tensor(
            O_kept.rearrange("p (t e) -> p t e", e=8),
            O_all.rearrange("p (t e) -> p t e", e=8),
            keep_sb[:].unsqueeze(2).broadcast_to((128, NSUB, 8)), ALU.mult),
            reads=["O_all", "keep"], writes=["O_kept"])
        sc.op("tensor", lambda e: e.matmul(
            PH0[:, 0:512], l128_sb[:], O_kept, start=True, stop=False,
            skip_group_check=True),
            reads=["l128", "O_kept"], writes=["PH0"])
        sc.op("tensor", lambda e: e.matmul(
            PSM[0:1, 24:536] if False else PSM[0:1, 0:512], onec_sb[:], O_kept,
            start=True, stop=True, skip_group_check=True),
            reads=["onec", "O_kept", "PSM"], writes=["PSM"])
        sc.op("vector", lambda e: e.tensor_copy(cnt_row2, PSM[0:1, 0:512]),
              reads=["PSM"], writes=["cntrow2"])
        sc.dma("bnc", lambda e: e.dma_start(
            out=cnt2_dram.ap().rearrange("a b -> (a b)")[None, :], in_=cnt_row2),
            reads=["cntrow2"], writes=["cntd2"])
        sc.dma("bnc", lambda e: e.dma_start(out=cnt64_2[:], in_=cnt2_dram.ap()),
               reads=["cntd2"], writes=["cnt642"])
        sc.op("tensor", lambda e: e.matmul(
            PSM[0:64, 24:32], l128_sb[0:64, 0:64], cnt64_2[:], start=True, stop=True,
            skip_group_check=True),
            reads=["l128", "cnt642", "PSM"], writes=["PSM"])
        sc.op("vector", lambda e: e.tensor_copy(to64_2[:], PSM[0:64, 24:32]),
              reads=["PSM"], writes=["to642"])
        sc.dma("bnc", lambda e: e.dma_start(out=to2_dram.ap(), in_=to64_2[:]),
               reads=["to642"], writes=["tod2"])
        sc.dma("bnc", lambda e: e.dma_start(
            out=to_row2, in_=to2_dram.ap().rearrange("a b -> (a b)")[None, :]),
            reads=["tod2"], writes=["torow2"])
        sc.op("tensor", lambda e: e.matmul(
            PH0[:, 0:512], onerf_sb[:], to_row2, start=False, stop=True,
            skip_group_check=True),
            reads=["onerf", "torow2", "PH0"], writes=["PH0"])
        sc.op("vector", lambda e: e.tensor_copy(RK_all, PH0[:, 0:512]),
              reads=["PH0"], writes=["RK"])
        sc.op("vector", lambda e: e.tensor_tensor(t1_sb, RK_all, O_kept, ALU.mult),
              reads=["RK", "O_kept"], writes=["t1"])
        sc.op("vector", lambda e: e.reduce_sum(
            krank_sb[:].rearrange("p (t o) -> p t o", o=1),
            t1_sb.rearrange("p (t e) -> p t e", e=8), AX.X),
            reads=["t1"], writes=["krank"])
        sc.op("vector", lambda e: e.tensor_tensor(
            t1_sb.rearrange("p (t e) -> p t e", e=8),
            O_all.rearrange("p (t e) -> p t e", e=8),
            off_bc[:].unsqueeze(1).broadcast_to((128, NSUB, 8)), ALU.mult),
            reads=["O_all", "offbc"], writes=["t1"])
        sc.op("vector", lambda e: e.reduce_sum(
            offown_sb[:].rearrange("p (t o) -> p t o", o=1),
            t1_sb.rearrange("p (t e) -> p t e", e=8), AX.X),
            reads=["t1"], writes=["offown"])
        sc.op("vector", lambda e: e.tensor_tensor(slotf_sb[:], offown_sb[:], krank_sb[:], ALU.add),
              reads=["offown", "krank"], writes=["slotf"])
        sc.op("vector", lambda e: e.select(slotd_f[:], nk_i[:], trash_sb[:], slotf_sb[:],
                                           add_drain=True),
              reads=["nk", "trash", "slotf"], writes=["slotdf"])
        sc.op("vector", lambda e: e.tensor_copy(slotd_i[:], slotd_f[:]),
              reads=["slotdf"], writes=["slotdi"])
        sc.op("vector", lambda e: e.select(slotf_sb[:], nk_i[:], ttok_sb[:], slotf_sb[:],
                                           add_drain=True),
              reads=["nk", "ttok", "slotf"], writes=["slotf"])
        sc.op("vector", lambda e: e.tensor_copy(slot_i[:], slotf_sb[:]),
              reads=["slotf"], writes=["sloti"])

        sc.op("vector", lambda e: e.memset(zrow_bf, 0.0), writes=["zrow"])
        ntile_all = TOT // 128
        sc.dma("zzf", lambda e, n=ntile_all: e.dma_start(
            out=gidx_dram.ap()[0:TOT, :].rearrange("(a b) o -> a (b o)", a=128),
            in_=zrow_bf[:, 0:2 * n].bitcast(I32)),
            reads=["zrow"], writes=["gidx0"])
        sc.dma("zzf", lambda e: e.dma_start(
            out=gidx_dram.ap()[TOT:TOT + 1, :],
            in_=zrow_bf[0:1, 0:2].bitcast(I32)),
            reads=["zrow"], writes=["gidx0b"])
        sc.dma("zzf", lambda e: e.dma_start(
            out=ye_buf.ap()[TOT:TOT + 1, :], in_=zrow_bf[0:1, 0:DM]),
            reads=["zrow"], writes=["yetrash"])

        for fg in range(2):
            wb = fg % 2
            for dh in range(2):
                sc.dma("w1", lambda e, fg=fg, wb=wb, dh=dh: e.dma_start(
                    out=w1g_bf[:, wb * 4096 + dh * 2048:
                               wb * 4096 + (dh + 1) * 2048].rearrange(
                        "p (dc f) -> p dc f", dc=4),
                    in_=w1_in[0].rearrange("(dc p) f -> p dc f", dc=8)
                        [:, dh * 4:(dh + 1) * 4, fg * 512:(fg + 1) * 512]),
                    writes=[f"w1g{wb}"] if dh == 1 else [f"w1g{wb}h"])
        for fc in range(32):
            sc.dma("w2", lambda e, fc=fc: e.dma_start(
                out=w2_sb[:, fc * 1024:(fc + 1) * 1024],
                in_=w2_in[0, fc * 128:(fc + 1) * 128, :]),
                writes=[f"w2{fc}"])

        sct_keys = []
        for stt in range(NSUB):
            sct_keys.append(sc.dma("sct", lambda e, stt=stt: e.indirect_dma_start(
                out=gidx_dram.ap(), out_offset=IndirectOffsetOnAxis(slot_i[:, stt:stt + 1], 0),
                in_=tokid_sb[:, stt:stt + 1], in_offset=None),
                reads=["tokid", "sloti"] + (["gidx0"] if stt == 0 else []),
                writes=[f"sctw{stt}"]))
        for g in range(TOT // 128):
            sc.dma("xsc", lambda e, g=g: e.dma_start(
                out=gidx_sb[:, g:g + 1],
                in_=gidx_dram.ap()[g * 128:(g + 1) * 128, :]),
                extra=(sct_keys if g == 0 else []), writes=[f"gidx{g}"])

        for ex in range(NE):
            chunks = []
            t0 = 0
            while t0 < nt[ex]:
                tl = min(8, nt[ex] - t0)
                chunks.append((t0, tl))
                t0 += tl
            for ci, (ct0, tl) in enumerate(chunks):
                gbase = offt[ex] // 128 + ct0
                for ti in range(tl):
                    sl = ti % 4
                    gg = gbase + ti
                    sc.dma("xet", lambda e, g=gg, sl=sl: e.indirect_dma_start(
                        out=xet_bf[:, sl * DM:(sl + 1) * DM], out_offset=None,
                        in_=x1b_in, in_offset=IndirectOffsetOnAxis(gidx_sb[:, g:g + 1], 0)),
                        reads=[f"gidx{gg}"], writes=[f"xet{sl}"])
                    for dc in range(8):
                        sc.op("tensor", lambda e, dc=dc, sl=sl: e.transpose(
                            TXC[:, dc * 128:(dc + 1) * 128],
                            xet_bf[:, sl * DM + dc * 128: sl * DM + (dc + 1) * 128],
                            idb_sb[:]),
                            reads=[f"xet{sl}", "idb"],
                            writes=["TXC"] if dc in (0, 7) else [])
                    sc.op("vector", lambda e, ti=ti: e.tensor_copy(
                        xe_fm[:].rearrange("p (dc t) -> p dc t", dc=8)[:, :, ti * 128:(ti + 1) * 128],
                        TXC[:].rearrange("p (dc t) -> p dc t", dc=8)),
                        reads=["TXC"], writes=[f"xefm{ti}"])
                xefm_keys = [f"xefm{ti}" for ti in range(tl)]
                nsl_total = tl * 128
                sgs = []
                o = 0
                while o < nsl_total:
                    sgs.append((o, min(512, nsl_total - o)))
                    o += 512

                for fg in range(8):
                    wb = fg % 2
                    if ex == 0 and ci == 0 and fg < 2:
                        pass
                    else:
                      for dh in range(2):
                        sc.dma("w1", lambda e, ex=ex, fg=fg, wb=wb, dh=dh: e.dma_start(
                            out=w1g_bf[:, wb * 4096 + dh * 2048:
                                       wb * 4096 + (dh + 1) * 2048].rearrange(
                                "p (dc f) -> p dc f", dc=4),
                            in_=w1_in[ex].rearrange("(dc p) f -> p dc f", dc=8)
                                [:, dh * 4:(dh + 1) * 4, fg * 512:(fg + 1) * 512]),
                            writes=[f"w1g{wb}"] if dh == 1 else [f"w1g{wb}h"])
                    for fsub in range(4):
                        fc = fg * 4 + fsub
                        for sgi, (so, sn) in enumerate(sgs):
                            ph = PH0 if (fc * len(sgs) + sgi) % 2 == 0 else PH1
                            phk = "PH0" if (fc * len(sgs) + sgi) % 2 == 0 else "PH1"
                            for dc in range(8):
                                sc.op("tensor",
                                      lambda e, wb=wb, fsub=fsub, dc=dc, so=so, sn=sn, ph=ph:
                                      e.matmul(
                                          ph[:, 0:sn],
                                          w1g_bf[:, wb * 4096 + dc * 512 + fsub * 128:
                                                 wb * 4096 + dc * 512 + (fsub + 1) * 128],
                                          xe_fm[:, dc * 1024 + so: dc * 1024 + so + sn],
                                          start=(dc == 0), stop=(dc == 7)),
                                      reads=[f"w1g{wb}", f"w1g{wb}h"] +
                                            (xefm_keys if dc in (0, 7) else []),
                                      writes=[phk] if dc in (0, 7) else [])
                            if (fc * len(sgs) + sgi) % 2 == 0:
                                sc.op("scalar", lambda e, fc=fc, so=so, sn=sn, ph=ph: e.activation(
                                    h_sb[:, fc * 1024 + so: fc * 1024 + so + sn],
                                    ph[:, 0:sn], AF.Relu),
                                    reads=[phk],
                                    writes=[f"h{fc}_{sgi}"] +
                                           (ALIAS_KEYS if (ex == 0 and ci == 0 and fc == 0 and sgi == 0)
                                            else []))
                            else:
                                sc.op("vector", lambda e, fc=fc, so=so, sn=sn, ph=ph:
                                      e.tensor_scalar_max(
                                          h_sb[:, fc * 1024 + so: fc * 1024 + so + sn],
                                          ph[:, 0:sn], 0.0),
                                      reads=[phk], writes=[f"h{fc}_{sgi}"])
                h_keys = [f"h{fc}_{sgi}" for fc in range(32) for sgi in range(len(sgs))]

                if ci == 0 and ex > 0:
                    for fc in range(32):
                        sc.dma("w2", lambda e, ex=ex, fc=fc: e.dma_start(
                            out=w2_sb[:, fc * 1024:(fc + 1) * 1024],
                            in_=w2_in[ex, fc * 128:(fc + 1) * 128, :]),
                            writes=[f"w2{fc}"])

                for ti in range(tl):
                    py = PY0 if ti % 2 == 0 else PY1
                    pyk = "PY0" if ti % 2 == 0 else "PY1"
                    for fc in range(32):
                        for half in range(2):
                            first = (fc == 0 and half == 0)
                            last = (fc == 31 and half == 1)
                            sc.op("tensor", lambda e, ti=ti, fc=fc, half=half, py=py: e.matmul(
                                py[:, half * 512:(half + 1) * 512],
                                h_sb[:, fc * 1024 + ti * 128: fc * 1024 + (ti + 1) * 128],
                                w2_sb[:, fc * 1024 + half * 512: fc * 1024 + (half + 1) * 512],
                                start=(fc == 0), stop=(fc == 31)),
                                reads=[f"w2{fc}", f"h{fc}_{(ti * 128) // 512}"] +
                                      (h_keys if first else []),
                                writes=[pyk] if (first or last) else [])
                    sl = ti % 2
                    for half in range(2):
                        sc.op("vector", lambda e, sl=sl, half=half, py=py: e.tensor_copy(
                            yet_bf[:, sl * DM + half * 512: sl * DM + (half + 1) * 512],
                            py[:, half * 512:(half + 1) * 512]),
                            reads=[pyk], writes=[f"yet{sl}_{half}"])
                    gg = gbase + ti
                    sc.dma("yeo", lambda e, g=gg, sl=sl: e.dma_start(
                        out=ye_buf.ap()[g * 128:(g + 1) * 128, :],
                        in_=yet_bf[:, sl * DM:(sl + 1) * DM]),
                        reads=[f"yet{sl}_0", f"yet{sl}_1"], writes=[f"yeo{gg}"])

        ye_keys = [f"yeo{g}" for g in range(TOT // 128)] + ["yetrash"]
        ye_keys_tok = [sc.wr[k] for k in ye_keys if k in sc.wr]

        t_outs = []
        for stt in range(NSUB):
            sl = stt % 3
            lb = 8 * sl
            sc.dma("yg", lambda e, stt=stt, sl=sl: e.indirect_dma_start(
                out=yg_bf[:, sl * DM:(sl + 1) * DM], out_offset=None,
                in_=ye_buf.ap(), in_offset=IndirectOffsetOnAxis(slotd_i[:, stt:stt + 1], 0)),
                reads=["slotdi"] + (ye_keys if stt == 0 else []),
                writes=[f"yg{sl}"])
            sc.dma("x1d", lambda e, stt=stt, sl=sl: e.dma_start(
                out=x1d_f[:, sl * DM:(sl + 1) * DM],
                in_=x1f_in[stt * 128:(stt + 1) * 128, :]),
                writes=[f"x1d{sl}"])
            sc.op("vector", lambda e, stt=stt, sl=sl, lb=lb: e.scalar_tensor_tensor(
                y2_f[:, sl * DM:(sl + 1) * DM], yg_bf[:, sl * DM:(sl + 1) * DM],
                s_all[:, stt:stt + 1], x1d_f[:, sl * DM:(sl + 1) * DM],
                ALU.mult, ALU.add, accum_out=ln_sm[:, lb + 0:lb + 1]),
                reads=[f"yg{sl}", f"x1d{sl}", "sall"],
                writes=[f"y2{sl}", f"lnsum{sl}"])
            sc.op("scalar", lambda e, sl=sl, lb=lb: e.activation(
                sq_scr[:, 0:DM], y2_f[:, sl * DM:(sl + 1) * DM], AF.Square,
                accum_out=ln_sm[:, lb + 1:lb + 2]),
                reads=[f"y2{sl}"], writes=[f"lnsq{sl}", "sq_alias"])
            sc.op("vector", lambda e, lb=lb: e.tensor_scalar_mul(
                ln_sm[:, lb + 2:lb + 3], ln_sm[:, lb + 0:lb + 1], 1.0 / DM),
                reads=[f"lnsum{sl}"], writes=[f"lnm{sl}"])
            sc.op("vector", lambda e, lb=lb: e.tensor_tensor(
                ln_sm[:, lb + 3:lb + 4], ln_sm[:, lb + 2:lb + 3], ln_sm[:, lb + 2:lb + 3],
                ALU.mult),
                reads=[f"lnm{sl}"], writes=[f"lnm2{sl}"])
            sc.op("vector", lambda e, lb=lb: e.scalar_tensor_tensor(
                ln_sm[:, lb + 4:lb + 5], ln_sm[:, lb + 1:lb + 2], 1.0 / DM,
                ln_sm[:, lb + 3:lb + 4], ALU.mult, ALU.subtract),
                reads=[f"lnsq{sl}", f"lnm2{sl}"], writes=[f"lnvar{sl}"])
            sc.op("scalar", lambda e, lb=lb: e.activation(
                ln_sm[:, lb + 5:lb + 6], ln_sm[:, lb + 4:lb + 5], AF.Ln, bias=eps_sb[:], scale=1.0),
                reads=[f"lnvar{sl}", "eps"], writes=[f"lnln{sl}"])
            sc.op("scalar", lambda e, lb=lb: e.activation(
                ln_sm[:, lb + 6:lb + 7], ln_sm[:, lb + 5:lb + 6], AF.Exp, bias=0.0, scale=-0.5),
                reads=[f"lnln{sl}"], writes=[f"lnrs{sl}"])
            sc.op("vector", lambda e, lb=lb: e.scalar_tensor_tensor(
                ln_sm[:, lb + 7:lb + 8], ln_sm[:, lb + 2:lb + 3], -1.0,
                ln_sm[:, lb + 6:lb + 7], ALU.mult, ALU.mult),
                reads=[f"lnm{sl}", f"lnrs{sl}"], writes=[f"lnmb{sl}"])
            sc.op("vector", lambda e, sl=sl, lb=lb: e.tensor_scalar(
                outt_f[:, sl * DM:(sl + 1) * DM], y2_f[:, sl * DM:(sl + 1) * DM],
                ln_sm[:, lb + 6:lb + 7], ln_sm[:, lb + 7:lb + 8],
                ALU.mult, ALU.add),
                reads=[f"y2{sl}", f"lnmb{sl}", f"lnrs{sl}"],
                writes=[f"outt{sl}", f"x1d{sl}"])
            t_outs.append(sc.dma("out", lambda e, stt=stt, sl=sl: e.dma_start(
                out=out_o[stt * 128:(stt + 1) * 128, :],
                in_=outt_f[:, sl * DM:(sl + 1) * DM]),
                reads=[f"outt{sl}", f"x1d{sl}"], writes=[f"outo{sl}"]))

        fin = list(t_outs)
        for stn in ("out", "yeo"):
            fin += sc.total(stn)
        fin += [("eng:scalar", sc.cnt["eng:scalar"]), ("eng:gpsimd", sc.cnt["eng:gpsimd"]),
                ("eng:vector", sc.cnt["eng:vector"]), ("eng:tensor", sc.cnt["eng:tensor"])]
        sc.op("sync", lambda e: e.nop(), extra=fin)

        sc.emit(block)
    return nc



_CACHE = {}


def _maskbias(bias_table):
    ws = WS
    coords = np.stack(np.meshgrid(np.arange(ws), np.arange(ws), indexing="ij"))
    cf = coords.reshape(2, -1)
    rel = cf[:, :, None] - cf[:, None, :]
    rel = rel.transpose(1, 2, 0).astype(np.int64)
    rel[..., 0] += ws - 1
    rel[..., 1] += ws - 1
    rel[..., 0] *= 2 * ws - 1
    idx = rel.sum(-1)[:ws, :ws]
    bias = np.asarray(bias_table, dtype=np.float32)[idx.reshape(-1)].reshape(ws, ws, H)
    mb = np.full((H, 128, 32), -1e9, dtype=np.float32)
    for h in range(H):
        for p in range(128):
            qw = (p % 32) // 8
            qi = p % 8
            mb[h, p, qw * 8:(qw + 1) * 8] = bias[qi, :, h]
    mb0 = np.concatenate([mb[2 * hp] for hp in range(8)], axis=1)
    mb1 = np.concatenate([mb[2 * hp + 1] for hp in range(8)], axis=1)
    return np.ascontiguousarray(mb0), np.ascontiguousarray(mb1)


def kernel(L0_attn_x, L0_attn_w_qkv, L0_attn_b_qkv, L0_attn_w_proj, L0_attn_b_proj,
           L0_attn_bias_table, L0_ffn_Wr, L0_ffn_W1, L0_ffn_W2):
    x = np.ascontiguousarray(np.asarray(L0_attn_x, dtype=np.float32))
    wqkv = np.ascontiguousarray(np.asarray(L0_attn_w_qkv, dtype=np.float32))
    bqkv = np.ascontiguousarray(np.asarray(L0_attn_b_qkv, dtype=np.float32))
    wproj = np.ascontiguousarray(np.asarray(L0_attn_w_proj, dtype=np.float32))
    bproj = np.ascontiguousarray(np.asarray(L0_attn_b_proj, dtype=np.float32))
    wr = np.ascontiguousarray(np.asarray(L0_ffn_Wr, dtype=np.float32))
    w1 = np.ascontiguousarray(np.asarray(L0_ffn_W1, dtype=np.float32))
    w2 = np.ascontiguousarray(np.asarray(L0_ffn_W2, dtype=np.float32))
    mb0, mb1 = _maskbias(L0_attn_bias_table)

    if "l1" not in _CACHE:
        _CACHE["l1"] = build_l1()
    nc1 = _CACHE["l1"]
    in_maps1 = []
    for c in range(NCORES):
        in_maps1.append({
            "x": np.ascontiguousarray(x[c * NSEQ:(c + 1) * NSEQ].reshape(T, DM)),
            "wqkv": wqkv, "bqkv": bqkv, "wproj": wproj, "bproj": bproj,
            "mb0": mb0, "mb1": mb1, "wr": wr,
        })
    res1 = run_bass_kernel_spmd(nc1, in_maps1, core_ids=list(range(NCORES)))

    cnts = np.ascontiguousarray(
        np.stack([np.asarray(res1.results[c]["tot"][0]) for c in range(NCORES)]).astype(np.float32))

    prefix = np.cumsum(cnts, axis=0) - cnts
    kept = np.clip(np.minimum(cnts, 8192.0 - prefix), 0.0, None)
    nt = tuple(int(max(1, int(np.ceil(kept[:, e].max() / 128.0)))) for e in range(NE))
    if ("l2", nt) not in _CACHE:
        _CACHE[("l2", nt)] = build_l2(nt)
    nc2 = _CACHE[("l2", nt)]
    in_maps2 = []
    for c in range(NCORES):
        r = res1.results[c]
        cmask = (np.arange(NE) < c).astype(np.float32).reshape(NE, 1)
        in_maps2.append({
            "x1f": np.ascontiguousarray(r["x1f"]),
            "x1b": np.ascontiguousarray(r["x1b"]),
            "eidx": np.ascontiguousarray(r["eidx"]),
            "ptok": np.ascontiguousarray(r["ptok"]),
            "rankl": np.ascontiguousarray(r["rankl"]),
            "cnts": cnts, "cmask": cmask, "w1": w1, "w2": w2,
        })
    res2 = run_bass_kernel_spmd(nc2, in_maps2, core_ids=list(range(NCORES)))

    out = np.concatenate(
        [np.asarray(res2.results[c]["out"]).reshape(NSEQ, SEQ, DM) for c in range(NCORES)],
        axis=0)
    return np.ascontiguousarray(out.astype(np.float32))


# version 18 (final)
# speedup vs baseline: 1.0083x; 1.0083x over previous
import numpy as np
import ml_dtypes
from contextlib import ExitStack

import concourse.bass as bass
import concourse.mybir as mybir
from concourse.bass import IndirectOffsetOnAxis
from concourse.bass_utils import run_bass_kernel_spmd

F32 = mybir.dt.float32
BF16 = mybir.dt.bfloat16
I32 = mybir.dt.int32
U32 = mybir.dt.uint32
AF = mybir.ActivationFunctionType
ALU = mybir.AluOpType
AX = mybir.AxisListType

NCORES = 8
NSEQ = 4
SEQ = 2048
DM = 1024
H = 16
WS = 8
T = NSEQ * SEQ
NBLK = 16
NSUB = 64
NE = 8
CLOC = 1280
TRASH = NE * CLOC
NTI = CLOC // 128
EPS = 1e-5

ENGS = ("sync", "vector", "scalar", "tensor", "gpsimd")


class Sched:

    def __init__(self, nc, stack):
        self.nc = nc
        self.stack = stack
        self.ops = {e: [] for e in ENGS}
        self.sems = {}
        self.cnt = {}
        for e in ENGS:
            key = "eng:" + e
            self.sems[key] = stack.enter_context(nc.semaphore(f"prog_{e}"))
            self.cnt[key] = 0
        self.eng_of_sem = {("eng:" + e): e for e in ENGS}
        self.wr = {}
        self.rd = {}
        self.streams = {}

    def stream(self, name, eng, k=4):
        slots = []
        for i in range(k):
            key = f"st:{name}:{i}"
            self.sems[key] = self.stack.enter_context(self.nc.semaphore(f"s_{name}_{i}"))
            self.cnt[key] = 0
            slots.append(key)
        self.streams[name] = {"eng": eng, "slots": slots, "i": 0}
        return name

    def _deps(self, eng, reads, writes, extra):
        deps = {}

        def need(tok):
            if tok is None:
                return
            dk, dv = tok
            if dv > deps.get(dk, 0):
                deps[dk] = dv

        for k in reads:
            need(self.wr.get(k))
        for k in writes:
            need(self.wr.get(k))
            for dk, dv in self.rd.get(k, {}).items():
                need((dk, dv))
        for tok in extra:
            need(tok)
        return deps

    def _book(self, tok, reads, writes):
        for k in reads:
            self.rd.setdefault(k, {})[tok[0]] = tok[1]
        for k in writes:
            self.wr[k] = tok
            self.rd[k] = {}
        return tok

    def op(self, eng, fn, reads=(), writes=(), extra=()):
        deps = self._deps(eng, reads, writes, extra)
        key = "eng:" + eng
        self.ops[eng].append((sorted(deps.items()), fn, (key, 1)))
        self.cnt[key] += 1
        return self._book((key, self.cnt[key]), reads, writes)

    def dma(self, stream, fn, reads=(), writes=(), extra=()):
        st = self.streams[stream]
        eng = st["eng"]
        key = st["slots"][st["i"] % len(st["slots"])]
        st["i"] += 1
        deps = self._deps(eng, reads, writes, extra)
        if self.cnt[key] > deps.get(key, 0):
            deps[key] = self.cnt[key]
        self.ops[eng].append((sorted(deps.items()), fn, (key, 16)))
        self.cnt[key] += 16
        return self._book((key, self.cnt[key]), reads, writes)

    def total(self, stream):
        st = self.streams[stream]
        return [(k, self.cnt[k]) for k in st["slots"] if self.cnt[k] > 0]

    def emit(self, block):
        for eng in ENGS:
            ops = self.ops[eng]
            if not ops:
                continue

            def make(eng, ops):
                def body(e):
                    hi = {}
                    for deps, fn, inc in ops:
                        for dk, dv in deps:
                            if dv > hi.get(dk, 0):
                                e.wait_ge(self.sems[dk], dv)
                                hi[dk] = dv
                        ik, iv = inc
                        fn(e).then_inc(self.sems[ik], iv)
                return body

            getattr(block, eng)(make(eng, ops))


def _rolled_pieces(j):
    if j < 15:
        return [(0, 4 + 128 * j, 128)]
    return [(0, 1924, 124), (124, 0, 4)]



L1_PHASES = "AB"
L1_STAGE = 99


def build_l1(debug=False):
    nc = bass.Bass("TRN2", target_bir_lowering=False)
    dbg = {}
    if debug:
        for name, shape, dt in (("d_qfm", [128, 4096], BF16), ("d_kfm", [128, 4096], BF16),
                                ("d_vbf", [128, 1024], BF16), ("d_ssb", [128, 512], F32),
                                ("d_pbf", [128, 512], BF16), ("d_ptbf", [128, 512], BF16),
                                ("d_atf", [128, 4096], BF16), ("d_y", [128, 1024], F32)):
            dbg[name] = nc.dram_tensor(name, shape, dt, kind="ExternalOutput").ap()

    x_in = nc.dram_tensor("x", [T, DM], F32, kind="ExternalInput").ap()
    wqkv_in = nc.dram_tensor("wqkv", [DM, 3 * DM], F32, kind="ExternalInput").ap()
    bqkv_in = nc.dram_tensor("bqkv", [3 * DM], F32, kind="ExternalInput").ap()
    wproj_in = nc.dram_tensor("wproj", [DM, DM], F32, kind="ExternalInput").ap()
    bproj_in = nc.dram_tensor("bproj", [DM], F32, kind="ExternalInput").ap()
    mb0_in = nc.dram_tensor("mb0", [128, 256], F32, kind="ExternalInput").ap()
    mb1_in = nc.dram_tensor("mb1", [128, 256], F32, kind="ExternalInput").ap()
    wr_in = nc.dram_tensor("wr", [DM, NE], F32, kind="ExternalInput").ap()

    x1f_out = nc.dram_tensor("x1f", [T, DM], F32, kind="ExternalOutput").ap()
    x1b_out = nc.dram_tensor("x1b", [T, DM], BF16, kind="ExternalOutput").ap()
    eidx_out = nc.dram_tensor("eidx", [128, NSUB], F32, kind="ExternalOutput").ap()
    p_out = nc.dram_tensor("ptok", [128, NSUB], F32, kind="ExternalOutput").ap()
    rankl_out = nc.dram_tensor("rankl", [128, NSUB], F32, kind="ExternalOutput").ap()
    tot_out = nc.dram_tensor("tot", [1, NE], F32, kind="ExternalOutput").ap()

    cnt_dram = nc.dram_tensor("cnt_scr", [NSUB, NE], F32, kind="Internal")
    to_dram = nc.dram_tensor("to_scr", [NSUB, NE], F32, kind="Internal")

    ident_bf_c = nc.inline_tensor(np.eye(128, dtype=ml_dtypes.bfloat16), "ident_bf")
    ident_f_c = nc.inline_tensor(np.eye(128, dtype=np.float32), "ident_f")
    l128_c = nc.inline_tensor(np.triu(np.ones((128, 128), dtype=np.float32), 1), "l128")
    onesc_c = nc.inline_tensor(np.ones((128, 1), dtype=np.float32), "onesc")
    onesrf_c = nc.inline_tensor(np.ones((1, 128), dtype=np.float32), "onesrf")
    onesrb_c = nc.inline_tensor(np.ones((1, 128), dtype=ml_dtypes.bfloat16), "onesrb")
    iota8_c = nc.inline_tensor(np.tile(np.arange(8, dtype=np.float32)[None, :], (128, 1)), "iota8")

    with ExitStack() as st:
        en = st.enter_context
        sc = Sched(nc, st)
        sc.stream("cst", "sync", 2)
        sc.stream("xl", "sync", 4)
        sc.stream("x1f", "sync", 4)
        sc.stream("bnc", "sync", 2)
        sc.stream("x1b", "gpsimd", 4)
        sc.stream("wl", "gpsimd", 4)
        sc.stream("x1r", "gpsimd", 4)

        wqkv_bf = en(nc.sbuf_tensor("wqkv_bf", [128, 8 * 3072], BF16))
        wproj_bf = en(nc.sbuf_tensor("wproj_bf", [128, 8 * 1024], BF16))
        wr_bf = en(nc.sbuf_tensor("wr_bf", [128, 8 * 8], BF16))
        bqkv_sb = en(nc.sbuf_tensor("bqkv_sb", [128, 24], F32))
        bq8_sb = en(nc.sbuf_tensor("bq8_sb", [128, 8], F32))
        bproj_row = en(nc.sbuf_tensor("bproj_row", [1, DM], BF16))
        bv_row = en(nc.sbuf_tensor("bv_row", [1, DM], BF16))
        mb0_sb = en(nc.sbuf_tensor("mb0_sb", [128, 256], F32))
        mb1_sb = en(nc.sbuf_tensor("mb1_sb", [128, 256], F32))
        idb_sb = en(nc.sbuf_tensor("idb_sb", [128, 128], BF16))
        idf_sb = en(nc.sbuf_tensor("idf_sb", [128, 128], F32))
        l128_sb = en(nc.sbuf_tensor("l128_sb", [128, 128], F32))
        onec_sb = en(nc.sbuf_tensor("onec_sb", [128, 1], F32))
        onerf_sb = en(nc.sbuf_tensor("onerf_sb", [1, 128], F32))
        onerb_sb = en(nc.sbuf_tensor("onerb_sb", [1, 128], BF16))
        iota8_sb = en(nc.sbuf_tensor("iota8_sb", [128, 8], F32))

        O_all = en(nc.sbuf_tensor("O_all", [128, NSUB * 8], F32))
        p_all = en(nc.sbuf_tensor("p_all", [128, NSUB], F32))
        eidx_all = en(nc.sbuf_tensor("eidx_all", [128, NSUB], F32))

        rankl_all = en(nc.sbuf_tensor("rankl_all", [128, NSUB], F32))
        cnt_row = en(nc.sbuf_tensor("cnt_row", [1, NSUB * 8], F32))
        cnt64 = en(nc.sbuf_tensor("cnt64", [NSUB, 8], F32))
        to64 = en(nc.sbuf_tensor("to64", [NSUB, 8], F32))
        to_row = en(nc.sbuf_tensor("to_row", [1, NSUB * 8], F32))
        tot_row = en(nc.sbuf_tensor("tot_row", [1, 8], F32))

        x_f32 = en(nc.sbuf_tensor("x_f32", [128, 8 * DM], F32))
        xtok_bf = en(nc.sbuf_tensor("xtok_bf", [128, 2 * DM], BF16))
        x_fm = en(nc.sbuf_tensor("x_fm", [128, 2 * 8 * 512], BF16))
        q_fm = en(nc.sbuf_tensor("q_fm", [128, 8 * 512], BF16))
        k_fm = en(nc.sbuf_tensor("k_fm", [128, 8 * 512], BF16))
        v_bf = en(nc.sbuf_tensor("v_bf", [128, 4 * DM], BF16))
        S_sb = en(nc.sbuf_tensor("S_sb", [128, 3 * 512], F32))
        trl_sb = S_sb[:, 0:512]
        P_bf = en(nc.sbuf_tensor("P_bf", [128, 3 * 512], BF16))
        Pt_bf = en(nc.sbuf_tensor("Pt_bf", [128, 3 * 512], BF16))
        RL_all = P_bf[:, 0:1024].bitcast(F32)
        negmx = en(nc.sbuf_tensor("negmx", [128, 3], F32))
        dn_sb = en(nc.sbuf_tensor("dn_sb", [128, 48], F32))
        r_sb = en(nc.sbuf_tensor("r_sb", [128, 48], F32))
        attn_fm = en(nc.sbuf_tensor("attn_fm", [128, 8 * 512], BF16))
        y_f32 = en(nc.sbuf_tensor("y_f32", [128, 2 * DM], F32))
        x1f_sb = en(nc.sbuf_tensor("x1f_sb", [128, 2 * DM], F32))
        x1b_sb = en(nc.sbuf_tensor("x1b_sb", [128, 2 * DM], BF16))
        ln_sm = en(nc.sbuf_tensor("ln_sm", [128, 16], F32))
        eps_sb = en(nc.sbuf_tensor("eps_sb", [128, 1], F32))
        x1r_bf = xtok_bf
        x1_fm = q_fm
        lg_sb = k_fm[0:8, 0:1024].bitcast(F32)
        ltok_sb = en(nc.sbuf_tensor("ltok_sb", [128, 2 * 8], F32))
        nm8_sb = en(nc.sbuf_tensor("nm8_sb", [128, 2], F32))
        e8_sb = en(nc.sbuf_tensor("e8_sb", [128, 2 * 8], F32))
        s8_sb = en(nc.sbuf_tensor("s8_sb", [128, 2], F32))
        mx8_sb = en(nc.sbuf_tensor("mx8_sb", [128, 2 * 8], F32))
        mi8_sb = en(nc.sbuf_tensor("mi8_sb", [128, 2 * 8], U32))

        TX = en(nc.psum_tensor("TX", [128, 1024], BF16))
        PB12 = en(nc.psum_tensor("PB12", [128, 1024], F32))
        PB34 = en(nc.psum_tensor("PB34", [128, 1024], F32))
        PB5 = en(nc.psum_tensor("PB5", [128, 512], F32))
        PB6 = en(nc.psum_tensor("PB6", [128, 512], F32))
        PB7 = en(nc.psum_tensor("PB7", [128, 512], F32))

        block = en(nc.Block())

        for dst, src, key in ((idb_sb, ident_bf_c, "idb"), (idf_sb, ident_f_c, "idf"),
                              (l128_sb, l128_c, "l128"), (onec_sb, onesc_c, "onec"),
                              (onerf_sb, onesrf_c, "onerf"), (onerb_sb, onesrb_c, "onerb"),
                              (iota8_sb, iota8_c, "iota8")):
            sc.dma("cst", lambda e, d=dst, s=src: e.dma_start(out=d[:], in_=s.ap()),
                   writes=[key])
        sc.dma("cst", lambda e: e.dma_start(out=mb0_sb[:], in_=mb0_in), writes=["mb0"])
        sc.dma("cst", lambda e: e.dma_start(out=mb1_sb[:], in_=mb1_in), writes=["mb1"])
        for dc in range(8):
            sc.dma("wl", lambda e, dc=dc: e.dma_start(
                out=wqkv_bf[:, dc * 3072:(dc + 1) * 3072],
                in_=wqkv_in[dc * 128:(dc + 1) * 128, :]), writes=[f"wq{dc}"])
        for c in range(8):
            sc.dma("wl", lambda e, c=c: e.dma_start(
                out=wproj_bf[:, c * 1024:(c + 1) * 1024],
                in_=wproj_in[c * 128:(c + 1) * 128, :]), writes=[f"wp{c}"])
        for dc in range(8):
            sc.dma("wl", lambda e, dc=dc: e.dma_start(
                out=wr_bf[:, dc * 8:(dc + 1) * 8],
                in_=wr_in[dc * 128:(dc + 1) * 128, :]), writes=[f"wr{dc}"])
        for c in range(24):
            sc.dma("wl", lambda e, c=c: e.dma_start(
                out=bqkv_sb[:, c:c + 1],
                in_=bqkv_in[c * 128:(c + 1) * 128].rearrange("(p o) -> p o", o=1)),
                writes=[f"bq{c}"])
        sc.dma("wl", lambda e: e.dma_start(out=bproj_row[:], in_=bproj_in[None, :]),
               writes=["bp"])
        sc.dma("wl", lambda e: e.dma_start(out=bv_row[:], in_=bqkv_in[2048:3072][None, :]),
               writes=["bv"])
        sc.op("vector", lambda e: e.tensor_scalar_mul(bq8_sb[:], bqkv_sb[:, 0:8], 0.125),
              reads=[f"bq{c}" for c in range(8)], writes=["bq8"])
        sc.op("vector", lambda e: e.memset(eps_sb[:], EPS), writes=["eps"])
        sc.op("vector", lambda e: e.memset(O_all[:], 0.0), writes=["O_all"])
        sc.op("vector", lambda e: e.memset(eidx_all[:], 0.0), writes=["eidx_all"])
        sc.op("vector", lambda e: e.memset(p_all[:], 0.0), writes=["p_all"])

        def stage_load(blk):
            seq = blk // 4
            for s in range(4):
                sg = blk * 4 + s
                j = sg % 16
                slot = sg % 8
                for dst0, src0, n in _rolled_pieces(j):
                    sc.dma("xl", lambda e, slot=slot, dst0=dst0, src0=src0, n=n, seq=seq:
                           e.dma_start(
                               out=x_f32[dst0:dst0 + n, slot * DM:(slot + 1) * DM],
                               in_=x_in[seq * SEQ + src0: seq * SEQ + src0 + n, :]),
                           writes=[f"xf{slot}"])

        def stage_fm(blk):
            buf = blk % 2
            for s in range(4):
                sg = blk * 4 + s
                slot = sg % 8
                cb = sg % 2
                sc.op("vector", lambda e, slot=slot, cb=cb: e.tensor_copy(
                    xtok_bf[:, cb * DM:(cb + 1) * DM],
                    x_f32[:, slot * DM:(slot + 1) * DM]),
                    reads=[f"xf{slot}"], writes=[f"xtok{cb}"])
                for dc in range(8):
                    sc.op("tensor", lambda e, dc=dc, cb=cb: e.transpose(
                        TX[:, dc * 128:(dc + 1) * 128],
                        xtok_bf[:, cb * DM + dc * 128: cb * DM + (dc + 1) * 128],
                        idb_sb[:]),
                        reads=[f"xtok{cb}", "idb"],
                        writes=["TX"] if dc in (0, 7) else [])
                sc.op("vector", lambda e, s=s, buf=buf: e.tensor_copy(
                    x_fm[:].rearrange("p (b dc t) -> p b dc t", b=2, dc=8)[:, buf, :, s * 128:(s + 1) * 128],
                    TX[:].rearrange("p (dc t) -> p dc t", dc=8)),
                    reads=["TX"], writes=[f"xfm{buf}_{s}"])

        stage_load(0)
        stage_fm(0)
        for blk in range(NBLK):
            buf = blk % 2
            seq = blk // 4
            xfm_keys = [f"xfm{buf}_{s}" for s in range(4)]

            if L1_STAGE < 2:
                continue
            for fc in range(16):
                half = fc % 2
                is_q = fc < 8
                c = fc if is_q else fc - 8
                for dc in range(8):
                    sc.op("tensor", lambda e, fc=fc, dc=dc, half=half, buf=buf: e.matmul(
                        PB12[:, half * 512:(half + 1) * 512],
                        wqkv_bf[:, dc * 3072 + fc * 128: dc * 3072 + (fc + 1) * 128],
                        x_fm[:, (buf * 8 + dc) * 512:(buf * 8 + dc + 1) * 512],
                        start=(dc == 0), stop=(dc == 7)),
                        reads=[f"wq{dc}"] + (xfm_keys if dc in (0, 7) else []),
                        writes=[f"PB12{half}"] if dc in (0, 7) else [])
                bias_ap = bq8_sb[:, c:c + 1] if is_q else bqkv_sb[:, 8 + c:9 + c]
                dst = q_fm if is_q else k_fm
                sc.op("scalar", lambda e, dst=dst, c=c, half=half, bias_ap=bias_ap,
                      scl=(0.125 if is_q else 1.0):
                      e.activation(dst[:, c * 512:(c + 1) * 512],
                                   PB12[:, half * 512:(half + 1) * 512], AF.Identity,
                                   bias=bias_ap, scale=scl),
                      reads=[f"PB12{half}", "bq8" if is_q else f"bq{8 + c}"],
                      writes=[("qfm" if is_q else "kfm") + str(c)])

            if L1_STAGE < 3:
                continue
            for s in range(4):
                for half in range(2):
                    sc.op("tensor", lambda e, half=half: e.matmul(
                        PB34[:, half * 512:(half + 1) * 512], onerb_sb[:],
                        bv_row[:, half * 512:(half + 1) * 512],
                        start=True, stop=False, skip_group_check=True),
                        reads=["onerb", "bv"], writes=[f"PB34{half}"])
                    for dc in range(8):
                        sc.op("tensor", lambda e, s=s, half=half, dc=dc, buf=buf: e.matmul(
                            PB34[:, half * 512:(half + 1) * 512],
                            x_fm[:, (buf * 8 + dc) * 512 + s * 128:(buf * 8 + dc) * 512 + (s + 1) * 128],
                            wqkv_bf[:, dc * 3072 + 2048 + half * 512: dc * 3072 + 2048 + (half + 1) * 512],
                            start=False, stop=(dc == 7), skip_group_check=True),
                            reads=[f"wq{dc}"] + (xfm_keys if dc in (0, 7) else []),
                            writes=[f"PB34{half}"] if dc == 7 else [])
                    sc.op("scalar", lambda e, s=s, half=half: e.activation(
                        v_bf[:, s * DM + half * 512: s * DM + (half + 1) * 512],
                        PB34[:, half * 512:(half + 1) * 512], AF.Identity,
                        bias=0.0, scale=1.0),
                        reads=[f"PB34{half}"], writes=[f"vbf{s}_{half}"])

            if blk + 1 < NBLK:
                stage_load(blk + 1)
                stage_fm(blk + 1)

            if L1_STAGE < 4:
                continue

            def emit_S(s):
                pp = s % 3
                for h in range(H):
                    hp = h // 2
                    for qg in range(4):
                        ps = PB5 if h % 2 == 0 else PB6
                        wkey = "PS0" if h % 2 == 0 else "PS1"
                        first = (qg == 0 and h < 2)
                        last = (qg == 3 and h >= 14)
                        sc.op("tensor", lambda e, h=h, hp=hp, qg=qg, s=s, ps=ps: e.matmul(
                            ps[qg * 32:(qg + 1) * 32, hp * 32:(hp + 1) * 32],
                            q_fm[(h % 2) * 64:(h % 2) * 64 + 64,
                                 hp * 512 + s * 128 + qg * 32: hp * 512 + s * 128 + (qg + 1) * 32],
                            k_fm[(h % 2) * 64:(h % 2) * 64 + 64,
                                 hp * 512 + s * 128 + qg * 32: hp * 512 + s * 128 + (qg + 1) * 32],
                            start=True, stop=True,
                            tile_position=((h % 2) * 64, qg * 32)),
                            reads=[f"qfm{hp}", f"kfm{hp}"],
                            writes=[wkey] if (first or last) else [])

            def emit_softmax(s):
                pp = s % 3
                sc.op("vector", lambda e, pp=pp: e.tensor_tensor(
                    S_sb[:].rearrange("p (b hp two g) -> p b hp two g", b=3, hp=8, two=2)[:, pp, :, 0, :],
                    PB5[:, 0:256].rearrange("p (hp g) -> p hp g", hp=8),
                    mb0_sb[:].rearrange("p (hp g) -> p hp g", hp=8), ALU.add),
                    reads=["PS0", "mb0"], writes=[f"S{pp}a"])
                sc.op("vector", lambda e, pp=pp: e.tensor_tensor(
                    S_sb[:].rearrange("p (b hp two g) -> p b hp two g", b=3, hp=8, two=2)[:, pp, :, 1, :],
                    PB6[:, 0:256].rearrange("p (hp g) -> p hp g", hp=8),
                    mb1_sb[:].rearrange("p (hp g) -> p hp g", hp=8), ALU.add),
                    reads=["PS1", "mb1"], writes=[f"S{pp}b"])
                sc.op("vector", lambda e, pp=pp: e.reduce_max(
                    negmx[:, pp:pp + 1], S_sb[:, pp * 512:(pp + 1) * 512], AX.X, negate=True),
                    reads=[f"S{pp}a", f"S{pp}b"], writes=[f"nm{pp}"])
                sc.op("scalar", lambda e, pp=pp: e.activation(
                    P_bf[:, pp * 512:(pp + 1) * 512], S_sb[:, pp * 512:(pp + 1) * 512],
                    AF.Exp, bias=negmx[:, pp:pp + 1], scale=1.0),
                    reads=[f"S{pp}a", f"S{pp}b", f"nm{pp}"], writes=[f"P{pp}"])
                sc.op("vector", lambda e, pp=pp: e.reduce_sum(
                    dn_sb[:, pp * 16:(pp + 1) * 16].rearrange("p (h o) -> p h o", o=1),
                    P_bf[:, pp * 512:(pp + 1) * 512].rearrange("p (h k) -> p h k", h=16), AX.X),
                    reads=[f"P{pp}"], writes=[f"dn{pp}"])
                sc.op("vector", lambda e, pp=pp: e.reciprocal(
                    r_sb[:, pp * 16:(pp + 1) * 16], dn_sb[:, pp * 16:(pp + 1) * 16]),
                    reads=[f"dn{pp}"], writes=[f"r{pp}"])
                sc.op("vector", lambda e, pp=pp: e.tensor_tensor(
                    P_bf[:, pp * 512:(pp + 1) * 512].rearrange("p (h k) -> p h k", h=16),
                    P_bf[:, pp * 512:(pp + 1) * 512].rearrange("p (h k) -> p h k", h=16),
                    r_sb[:, pp * 16:(pp + 1) * 16].unsqueeze(2).broadcast_to((128, 16, 32)),
                    ALU.mult),
                    reads=[f"P{pp}", f"r{pp}"], writes=[f"P{pp}"])
                sc.op("vector", lambda e, pp=pp: e.transpose(
                    Pt_bf[:, pp * 512:(pp + 1) * 512], P_bf[:, pp * 512:(pp + 1) * 512]),
                    reads=[f"P{pp}"], writes=[f"Pt{pp}"])

            def emit_PV(s):
                pp = s % 3
                pvkeys = ["PB120", "PB121"]
                for hv in range(2):
                    for h in range(H):
                        hp = h // 2
                        for qgl in range(2):
                            qg = hv * 2 + qgl
                            first = (h == 0 and qgl == 0)
                            last = (h == 15 and qgl == 1)
                            off = qgl * 512 + hp * 64 + hv * 32
                            sc.op("tensor", lambda e, h=h, qg=qg, s=s, pp=pp, off=off:
                                  e.matmul(
                                      PB12[(h % 2) * 64:(h % 2) * 64 + 64, off:off + 32],
                                      v_bf[qg * 32:(qg + 1) * 32,
                                           s * DM + h * 64: s * DM + (h + 1) * 64],
                                      Pt_bf[qg * 32:(qg + 1) * 32,
                                            pp * 512 + h * 32: pp * 512 + (h + 1) * 32],
                                      start=True, stop=True,
                                      tile_position=(qg * 32, (h % 2) * 64)),
                                  reads=[f"Pt{pp}", f"vbf{s}_0", f"vbf{s}_1"],
                                  writes=(pvkeys if (first or last) else []))
                for qgl in range(2):
                    sc.op("scalar", lambda e, s=s, qgl=qgl: e.activation(
                        attn_fm[:].rearrange("p (hp st qg2 qgl qi) -> p st qgl hp qg2 qi",
                                             hp=8, st=4, qg2=2, qgl=2)[:, s, qgl, :, :, :],
                        PB12[:, qgl * 512:(qgl + 1) * 512]
                        .rearrange("p (hp hv qi) -> p hp hv qi", hp=8, hv=2),
                        AF.Identity, bias=0.0, scale=1.0),
                        reads=pvkeys, writes=[f"atf{s}_{qgl}"])

            def emit_proj(s):
                sg = blk * 4 + s
                slot = sg % 8
                cb = sg % 2
                lb = 8 * (s % 2)
                lp = s % 2
                seq_l = blk // 4
                atf_keys = [f"atf{s}_0", f"atf{s}_1"]
                sc.op("tensor", lambda e: e.matmul(
                    PB34[:, 0:512], onerb_sb[:], bproj_row[:, 0:512],
                    start=True, stop=False, skip_group_check=True),
                    reads=["onerb", "bp"] + atf_keys, writes=["PB340"])
                sc.op("tensor", lambda e: e.matmul(
                    PB34[:, 512:1024], onerb_sb[:], bproj_row[:, 512:1024],
                    start=True, stop=False, skip_group_check=True),
                    reads=["onerb", "bp"], writes=["PB341"])
                for c in range(8):
                    for half in range(2):
                        sc.op("tensor", lambda e, c=c, half=half, s=s: e.matmul(
                            PB34[:, half * 512:(half + 1) * 512],
                            attn_fm[:, c * 512 + s * 128: c * 512 + (s + 1) * 128],
                            wproj_bf[:, c * 1024 + half * 512: c * 1024 + (half + 1) * 512],
                            start=False, stop=(c == 7), skip_group_check=True),
                            reads=[f"wp{c}"] + atf_keys,
                            writes=[f"PB34{half}"] if c == 7 else [])
                for half in range(2):
                    sc.op("vector", lambda e, slot=slot, cb=cb, half=half: e.tensor_tensor(
                        y_f32[:, cb * DM + half * 512: cb * DM + (half + 1) * 512],
                        PB34[:, half * 512:(half + 1) * 512],
                        x_f32[:, slot * DM + half * 512: slot * DM + (half + 1) * 512],
                        ALU.add),
                        reads=[f"PB34{half}", f"xf{slot}"],
                        writes=[f"y{cb}_{half}"])
                ykeys = [f"y{cb}_0", f"y{cb}_1"]
                sc.op("scalar", lambda e, cb=cb, lb=lb: e.activation(
                    x1f_sb[:, cb * DM:(cb + 1) * DM], y_f32[:, cb * DM:(cb + 1) * DM],
                    AF.Identity, bias=0.0, scale=1.0,
                    accum_out=ln_sm[:, lb + 0:lb + 1]),
                    reads=ykeys, writes=[f"lnsum{lp}", f"x1f{cb}"])
                sc.op("scalar", lambda e, cb=cb, lb=lb: e.activation(
                    x1f_sb[:, cb * DM:(cb + 1) * DM], y_f32[:, cb * DM:(cb + 1) * DM], AF.Square,
                    accum_out=ln_sm[:, lb + 1:lb + 2]),
                    reads=ykeys, writes=[f"lnsq{lp}", f"x1f{cb}"])
                sc.op("vector", lambda e, lb=lb: e.tensor_scalar_mul(
                    ln_sm[:, lb + 2:lb + 3], ln_sm[:, lb + 0:lb + 1], 1.0 / DM),
                    reads=[f"lnsum{lp}"], writes=[f"lnm{lp}"])
                sc.op("vector", lambda e, lb=lb: e.tensor_tensor(
                    ln_sm[:, lb + 3:lb + 4], ln_sm[:, lb + 2:lb + 3], ln_sm[:, lb + 2:lb + 3],
                    ALU.mult),
                    reads=[f"lnm{lp}"], writes=[f"lnm2{lp}"])
                sc.op("vector", lambda e, lb=lb: e.scalar_tensor_tensor(
                    ln_sm[:, lb + 4:lb + 5], ln_sm[:, lb + 1:lb + 2], 1.0 / DM,
                    ln_sm[:, lb + 3:lb + 4], ALU.mult, ALU.subtract),
                    reads=[f"lnsq{lp}", f"lnm2{lp}"], writes=[f"lnvar{lp}"])
                sc.op("scalar", lambda e, lb=lb: e.activation(
                    ln_sm[:, lb + 5:lb + 6], ln_sm[:, lb + 4:lb + 5], AF.Ln, bias=eps_sb[:], scale=1.0),
                    reads=[f"lnvar{lp}", "eps"], writes=[f"lnln{lp}"])
                sc.op("scalar", lambda e, lb=lb: e.activation(
                    ln_sm[:, lb + 6:lb + 7], ln_sm[:, lb + 5:lb + 6], AF.Exp, bias=0.0, scale=-0.5),
                    reads=[f"lnln{lp}"], writes=[f"lnrs{lp}"])
                sc.op("vector", lambda e, lb=lb: e.scalar_tensor_tensor(
                    ln_sm[:, lb + 7:lb + 8], ln_sm[:, lb + 2:lb + 3], -1.0,
                    ln_sm[:, lb + 6:lb + 7], ALU.mult, ALU.mult),
                    reads=[f"lnm{lp}", f"lnrs{lp}"], writes=[f"lnmb{lp}"])
                sc.op("scalar", lambda e, cb=cb, lb=lb: e.activation(
                    x1f_sb[:, cb * DM:(cb + 1) * DM], y_f32[:, cb * DM:(cb + 1) * DM], AF.Identity,
                    bias=ln_sm[:, lb + 7:lb + 8], scale=ln_sm[:, lb + 6:lb + 7]),
                    reads=ykeys + [f"lnmb{lp}", f"lnrs{lp}"], writes=[f"x1f{cb}"])
                j = sg % 16
                for dst0, src0, n in _rolled_pieces(j):
                    sc.dma("x1f", lambda e, cb=cb, dst0=dst0, src0=src0, n=n, seq=seq_l:
                           e.dma_start(
                               out=x1f_out[seq * SEQ + src0: seq * SEQ + src0 + n, :],
                               in_=x1f_sb[dst0:dst0 + n, cb * DM:(cb + 1) * DM]),
                           reads=[f"x1f{cb}"], writes=[f"x1fo{sg}"])
                    sc.dma("x1b", lambda e, cb=cb, dst0=dst0, src0=src0, n=n, seq=seq_l:
                           e.dma_start(
                               out=x1b_out[seq * SEQ + src0: seq * SEQ + src0 + n, :],
                               in_=x1f_sb[dst0:dst0 + n, cb * DM:(cb + 1) * DM]),
                           reads=[f"x1f{cb}"], writes=[f"x1bo{sg}"])

            emit_S(0); emit_softmax(0)
            emit_S(1); emit_softmax(1)
            emit_S(2); emit_softmax(2)
            emit_PV(0)
            emit_S(3); emit_softmax(3)
            emit_PV(1)
            emit_proj(0)
            emit_PV(2)
            emit_proj(1)
            emit_PV(3)
            emit_proj(2)
            emit_proj(3)

        skip_a2 = L1_PHASES == "A"
        all_x1b = [f"x1bo{sg}" for sg in range(NSUB)]
        for rb in range(0 if skip_a2 else NBLK):
            for dc in range(8):
                sc.dma("xl", lambda e, rb=rb, dc=dc: e.dma_start_transpose(
                    x1_fm[:, dc * 512:(dc + 1) * 512],
                    x1b_out[rb * 512:(rb + 1) * 512, dc * 128:(dc + 1) * 128]),
                    reads=(all_x1b if (rb == 0 and dc == 0) else []),
                    writes=[f"x1fmc{dc}"] + ([f"qfm{c}" for c in range(8)]
                                             if (rb == 0 and dc == 0) else []))
            x1fm_keys = [f"x1fmc{dc}" for dc in range(8)]
            for dc in range(8):
                sc.op("tensor", lambda e, dc=dc: e.matmul(
                    PB7[0:8, 0:512],
                    wr_bf[:, dc * 8:(dc + 1) * 8],
                    x1_fm[:, dc * 512:(dc + 1) * 512],
                    start=(dc == 0), stop=(dc == 7)),
                    reads=[f"wr{dc}", f"x1fmc{dc}"],
                    writes=["PB7"] if dc in (0, 7) else [])
            sc.op("vector", lambda e: e.tensor_copy(lg_sb[:], PB7[0:8, 0:512]),
                  reads=["PB7"],
                  writes=["lg"] + ([f"kfm{c}" for c in range(8)] if rb == 0 else []))
            for s in range(4):
                st_i = rb * 4 + s
                pp = s % 2
                sc.op("tensor", lambda e, s=s: e.transpose(
                    PB5[:, 0:8], lg_sb[0:8, s * 128:(s + 1) * 128], idf_sb[0:8, 0:8]),
                    reads=["lg", "idf"], writes=["PS0"])
                sc.op("vector", lambda e, pp=pp: e.tensor_copy(
                    ltok_sb[:, pp * 8:(pp + 1) * 8], PB5[:, 0:8]),
                    reads=["PS0"], writes=[f"lt{pp}"])
                sc.op("vector", lambda e, pp=pp: e.reduce_max(
                    nm8_sb[:, pp:pp + 1], ltok_sb[:, pp * 8:(pp + 1) * 8], AX.X, negate=True),
                    reads=[f"lt{pp}"], writes=[f"nm8{pp}"])
                sc.op("scalar", lambda e, pp=pp: e.activation(
                    e8_sb[:, pp * 8:(pp + 1) * 8], ltok_sb[:, pp * 8:(pp + 1) * 8],
                    AF.Exp, bias=nm8_sb[:, pp:pp + 1], scale=1.0,
                    accum_out=s8_sb[:, pp:pp + 1]),
                    reads=[f"lt{pp}", f"nm8{pp}"], writes=[f"s8{pp}", f"e8{pp}"]),
                sc.op("vector", lambda e, st_i=st_i, pp=pp: e.reciprocal(
                    p_all[:, st_i:st_i + 1], s8_sb[:, pp:pp + 1]),
                    reads=[f"s8{pp}"], writes=["p_all"])
                sc.op("vector", lambda e, pp=pp: e.max(
                    mx8_sb[:, pp * 8:(pp + 1) * 8], ltok_sb[:, pp * 8:(pp + 1) * 8]),
                    reads=[f"lt{pp}"], writes=[f"mx8{pp}"])
                sc.op("vector", lambda e, pp=pp: e.max_index(
                    mi8_sb[:, pp * 8:(pp + 1) * 8], mx8_sb[:, pp * 8:(pp + 1) * 8],
                    ltok_sb[:, pp * 8:(pp + 1) * 8]),
                    reads=[f"mx8{pp}", f"lt{pp}"], writes=[f"mi8{pp}"])
                sc.op("vector", lambda e, st_i=st_i, pp=pp: e.tensor_copy(
                    eidx_all[:, st_i:st_i + 1], mi8_sb[:, pp * 8:pp * 8 + 1]),
                    reads=[f"mi8{pp}"], writes=["eidx_all"])
                sc.op("vector", lambda e, st_i=st_i: e.tensor_tensor(
                    O_all[:, st_i * 8:(st_i + 1) * 8], iota8_sb[:],
                    eidx_all[:, st_i:st_i + 1].broadcast_to((128, 8)), ALU.is_equal),
                    reads=["eidx_all", "iota8"], writes=["O_all"])

        if skip_a2:
            fin = []
            for stn in ("x1f", "x1b"):
                fin += sc.total(stn)
            fin += [("eng:scalar", sc.cnt["eng:scalar"]), ("eng:gpsimd", sc.cnt["eng:gpsimd"]),
                    ("eng:vector", sc.cnt["eng:vector"]), ("eng:tensor", sc.cnt["eng:tensor"])]
            sc.op("sync", lambda e: e.nop(), extra=fin)
            sc.emit(block)
            return nc
        sc.op("tensor", lambda e: e.matmul(
            PB12[:, 0:512], l128_sb[:], O_all[:], start=True, stop=False,
            skip_group_check=True),
            reads=["l128", "O_all"], writes=["PB120", "PB121"])
        sc.op("tensor", lambda e: e.matmul(
            PB7[0:1, 0:512], onec_sb[:], O_all[:], start=True, stop=True,
            skip_group_check=True),
            reads=["onec", "O_all"], writes=["PB7"])
        sc.op("vector", lambda e: e.tensor_copy(cnt_row[:], PB7[0:1, 0:512]),
              reads=["PB7"], writes=["cntrow"])
        sc.dma("bnc", lambda e: e.dma_start(
            out=cnt_dram.ap().rearrange("a b -> (a b)")[None, :], in_=cnt_row[:]),
            reads=["cntrow"], writes=["cntd"])
        sc.dma("bnc", lambda e: e.dma_start(out=cnt64[:], in_=cnt_dram.ap()),
               reads=["cntd"], writes=["cnt64"])
        sc.op("tensor", lambda e: e.matmul(
            PB6[0:64, 0:8], l128_sb[0:64, 0:64], cnt64[:], start=True, stop=True,
            skip_group_check=True),
            reads=["l128", "cnt64"], writes=["PS1"])
        sc.op("vector", lambda e: e.tensor_copy(to64[:], PB6[0:64, 0:8]),
              reads=["PS1"], writes=["to64"])
        sc.op("tensor", lambda e: e.matmul(
            PB6[0:1, 8:16], onec_sb[0:64, :], cnt64[:], start=True, stop=True,
            skip_group_check=True),
            reads=["onec", "cnt64"], writes=["PS1"])
        sc.op("vector", lambda e: e.tensor_copy(tot_row[:], PB6[0:1, 8:16]),
              reads=["PS1"], writes=["totrow"])
        t_tot = sc.dma("bnc", lambda e: e.dma_start(out=tot_out, in_=tot_row[:]),
                       reads=["totrow"], writes=["toto"])
        sc.dma("bnc", lambda e: e.dma_start(out=to_dram.ap(), in_=to64[:]),
               reads=["to64"], writes=["tod"])
        sc.dma("bnc", lambda e: e.dma_start(
            out=to_row[:], in_=to_dram.ap().rearrange("a b -> (a b)")[None, :]),
            reads=["tod"], writes=["torow"])
        sc.op("tensor", lambda e: e.matmul(
            PB12[:, 0:512], onerf_sb[:], to_row[:], start=False, stop=True,
            skip_group_check=True),
            reads=["onerf", "torow", "PB120", "PB121"], writes=["PB120", "PB121"])
        sc.op("vector", lambda e: e.tensor_copy(RL_all, PB12[:, 0:512]),
              reads=["PB120", "PB121"], writes=["RL", "P0", "P1"])
        sc.op("vector", lambda e: e.tensor_tensor(trl_sb, RL_all, O_all[:], ALU.mult),
              reads=["RL", "O_all"], writes=["trl", "S0a", "S0b", "S1a", "S1b"])
        sc.op("vector", lambda e: e.reduce_sum(
            rankl_all[:].rearrange("p (t o) -> p t o", o=1),
            trl_sb.rearrange("p (t e) -> p t e", e=8), AX.X),
            reads=["trl"], writes=["rankl"])
        t_o1 = sc.dma("bnc", lambda e: e.dma_start(out=rankl_out, in_=rankl_all[:]),
                      reads=["rankl"], writes=["o1"])
        t_o2 = sc.dma("bnc", lambda e: e.dma_start(out=eidx_out, in_=eidx_all[:]),
                      reads=["eidx_all"], writes=["o2"])
        t_o3 = sc.dma("bnc", lambda e: e.dma_start(out=p_out, in_=p_all[:]),
                      reads=["p_all"], writes=["o3"])

        fin = [t_o1, t_o2, t_o3, t_tot]
        for stn in ("x1f", "x1b", "bnc"):
            fin += sc.total(stn)
        fin += [("eng:scalar", sc.cnt["eng:scalar"]), ("eng:gpsimd", sc.cnt["eng:gpsimd"]),
                ("eng:vector", sc.cnt["eng:vector"]), ("eng:tensor", sc.cnt["eng:tensor"])]
        sc.op("sync", lambda e: e.nop(), extra=fin)

        sc.emit(block)
    return nc



def build_l2(nt):
    nc = bass.Bass("TRN2", target_bir_lowering=False, detect_race_conditions=False)

    x1f_in = nc.dram_tensor("x1f", [T, DM], F32, kind="ExternalInput").ap()
    x1b_in = nc.dram_tensor("x1b", [T, DM], BF16, kind="ExternalInput").ap()
    eidx_in = nc.dram_tensor("eidx", [128, NSUB], F32, kind="ExternalInput").ap()
    p_in = nc.dram_tensor("ptok", [128, NSUB], F32, kind="ExternalInput").ap()
    rankl_in = nc.dram_tensor("rankl", [128, NSUB], F32, kind="ExternalInput").ap()
    cnts_in = nc.dram_tensor("cnts", [NE, NE], F32, kind="ExternalInput").ap()
    cmask_in = nc.dram_tensor("cmask", [NE, 1], F32, kind="ExternalInput").ap()
    w1_in = nc.dram_tensor("w1", [NE, DM, 4096], F32, kind="ExternalInput").ap()
    w2_in = nc.dram_tensor("w2", [NE, 4096, DM], F32, kind="ExternalInput").ap()

    out_o = nc.dram_tensor("out", [T, DM], F32, kind="ExternalOutput").ap()

    offt = [0]
    for e in range(NE):
        offt.append(offt[-1] + nt[e] * 128)
    TOT = offt[NE]

    gidx_dram = nc.dram_tensor("gidx_scr", [TOT + T, 1], I32, kind="Internal")
    ye_buf = nc.dram_tensor("ye_buf", [TOT + 1, DM], BF16, kind="Internal")
    cnt2_dram = nc.dram_tensor("cnt2_scr", [NSUB, NE], F32, kind="Internal")
    to2_dram = nc.dram_tensor("to2_scr", [NSUB, NE], F32, kind="Internal")

    ident_bf_c = nc.inline_tensor(np.eye(128, dtype=ml_dtypes.bfloat16), "ident_bf")
    onesrf_c = nc.inline_tensor(np.ones((1, 128), dtype=np.float32), "onesrf")
    onesc_c = nc.inline_tensor(np.ones((128, 1), dtype=np.float32), "onesc")
    l128_c = nc.inline_tensor(np.triu(np.ones((128, 128), dtype=np.float32), 1), "l128")
    iota8_c = nc.inline_tensor(
        np.tile(np.arange(8, dtype=np.float32)[None, :], (128, 1)), "iota8")
    trash_c = nc.inline_tensor(np.full((128, NSUB), float(TOT), dtype=np.float32), "trash")
    ttok_c = nc.inline_tensor(
        (float(TOT) + np.arange(NSUB, dtype=np.float32)[None, :] * 128
         + np.arange(128, dtype=np.float32)[:, None]).astype(np.float32), "ttok")
    tokid_c = nc.inline_tensor(
        (np.arange(NSUB, dtype=np.int32)[None, :] * 128
         + np.arange(128, dtype=np.int32)[:, None]).astype(np.int32), "tokid")
    offr_c = nc.inline_tensor(np.array([[float(v) for v in offt[:NE]]], dtype=np.float32), "offr")

    with ExitStack() as st:
        en = st.enter_context
        sc = Sched(nc, st)
        sc.stream("cst", "sync", 2)
        sc.stream("xsc", "sync", 4)
        sc.stream("xet", "gpsimd", 4)
        sc.stream("zzf", "sync", 2)
        sc.stream("x1d", "sync", 4)
        sc.stream("zz", "gpsimd", 4)
        sc.stream("sct", "gpsimd", 16)
        sc.stream("w1", "gpsimd", 4)
        sc.stream("w2", "gpsimd", 4)
        sc.stream("yg", "gpsimd", 4)
        sc.stream("yeo", "scalar", 4)
        sc.stream("out", "scalar", 4)
        sc.stream("bnc", "sync", 2)

        idb_sb = en(nc.sbuf_tensor("idb_sb", [128, 128], BF16))
        onerf_sb = en(nc.sbuf_tensor("onerf_sb", [1, 128], F32))
        onec_sb = en(nc.sbuf_tensor("onec_sb", [128, 1], F32))
        l128_sb = en(nc.sbuf_tensor("l128_sb", [128, 128], F32))
        iotat_sb = en(nc.sbuf_tensor("iotat_sb", [128, 8], F32))
        trash_sb = en(nc.sbuf_tensor("trash_sb", [128, NSUB], F32))
        offr_sb = en(nc.sbuf_tensor("offr_sb", [1, NE], F32))
        eidx_sb = en(nc.sbuf_tensor("eidx_sb", [128, NSUB], F32))
        p_sb = en(nc.sbuf_tensor("p_sb", [128, NSUB], F32))
        rankl_sb = en(nc.sbuf_tensor("rankl_sb", [128, NSUB], F32))
        cnts_sb = en(nc.sbuf_tensor("cnts_sb", [NE, NE], F32))
        cmask_sb = en(nc.sbuf_tensor("cmask_sb", [NE, 1], F32))
        co_row = en(nc.sbuf_tensor("co_row", [1, NE], F32))
        co_bc = en(nc.sbuf_tensor("co_bc", [128, NE], F32))
        off_bc = en(nc.sbuf_tensor("off_bc", [128, NE], F32))
        co_own = en(nc.sbuf_tensor("co_own", [128, NSUB], F32))
        rankg_sb = en(nc.sbuf_tensor("rankg_sb", [128, NSUB], F32))
        keep_sb = en(nc.sbuf_tensor("keep_sb", [128, NSUB], F32))
        s_all = en(nc.sbuf_tensor("s_all", [128, NSUB], F32))
        nk_i = en(nc.sbuf_tensor("nk_i", [128, NSUB], I32))
        krank_sb = en(nc.sbuf_tensor("krank_sb", [128, NSUB], F32))
        offown_sb = en(nc.sbuf_tensor("offown_sb", [128, NSUB], F32))
        slotf_sb = en(nc.sbuf_tensor("slotf_sb", [128, NSUB], F32))
        slot_i = en(nc.sbuf_tensor("slot_i", [128, NSUB], I32))
        tokid_sb = en(nc.sbuf_tensor("tokid_sb", [128, NSUB], I32))
        ttok_sb = en(nc.sbuf_tensor("ttok_sb", [128, NSUB], F32))
        slotd_f = en(nc.sbuf_tensor("slotd_f", [128, NSUB], F32))
        slotd_i = en(nc.sbuf_tensor("slotd_i", [128, NSUB], I32))
        gidx_sb = en(nc.sbuf_tensor("gidx_sb", [128, (TOT // 128) + 1], I32))
        cnt64_2 = en(nc.sbuf_tensor("cnt64_2", [NSUB, 8], F32))
        to64_2 = en(nc.sbuf_tensor("to64_2", [NSUB, 8], F32))

        xet_bf = en(nc.sbuf_tensor("xet_bf", [128, 2 * DM], BF16))
        xe_fm = en(nc.sbuf_tensor("xe_fm", [128, 8 * 1024], BF16))
        w1g_bf = en(nc.sbuf_tensor("w1g_bf", [128, 2 * 4096], BF16))
        w2_sb = en(nc.sbuf_tensor("w2_sb", [128, 32 * 1024], BF16))
        h_sb = en(nc.sbuf_tensor("h_sb", [128, 32 * 1024], BF16))
        yet_bf = en(nc.sbuf_tensor("yet_bf", [128, 2 * DM], BF16))
        yg_bf = en(nc.sbuf_tensor("yg_bf", [128, 3 * DM], BF16))
        x1d_f = en(nc.sbuf_tensor("x1d_f", [128, 3 * DM], F32))
        ln_sm = en(nc.sbuf_tensor("ln_sm", [128, 24], F32))
        eps_sb = en(nc.sbuf_tensor("eps_sb", [128, 1], F32))
        O_all = h_sb[:, 1024:2048].bitcast(F32)
        O_kept = h_sb[:, 2048:3072].bitcast(F32)
        t1_sb = h_sb[:, 3072:4096].bitcast(F32)
        zrow_bf = h_sb[:, 4096:5120]
        RK_all = h_sb[:, 5120:6144].bitcast(F32)
        sq_scr = h_sb[:, 6144:8192].bitcast(F32)
        cnt_row2 = h_sb[0:1, 8192:9216].bitcast(F32)
        to_row2 = h_sb[0:1, 9216:10240].bitcast(F32)
        y2_f = h_sb[:, 12288:18432].bitcast(F32)
        outt_f = x1d_f
        ALIAS_KEYS = ["O_all", "O_kept", "t1", "zrow", "RK", "sq_alias",
                      "cntrow2", "torow2"]

        TXC = en(nc.psum_tensor("TXC", [128, 1024], BF16))
        PH0 = en(nc.psum_tensor("PH0", [128, 512], F32))
        PH1 = en(nc.psum_tensor("PH1", [128, 512], F32))
        PY0 = en(nc.psum_tensor("PY0", [128, 1024], F32))
        PY1 = en(nc.psum_tensor("PY1", [128, 1024], F32))
        PSM = en(nc.psum_tensor("PSM", [128, 512], F32))

        block = en(nc.Block())

        for dst, src, key in ((idb_sb, ident_bf_c.ap(), "idb"),
                              (onerf_sb, onesrf_c.ap(), "onerf"),
                              (onec_sb, onesc_c.ap(), "onec"),
                              (l128_sb, l128_c.ap(), "l128"),
                              (iotat_sb, iota8_c.ap(), "iotat"),
                              (trash_sb, trash_c.ap(), "trash"),
                              (offr_sb, offr_c.ap(), "offr"),
                              (eidx_sb, eidx_in, "eidx"), (p_sb, p_in, "p"),
                              (rankl_sb, rankl_in, "rankl"), (cnts_sb, cnts_in, "cnts"),
                              (cmask_sb, cmask_in, "cmask"), (tokid_sb, tokid_c.ap(), "tokid"),
                              (ttok_sb, ttok_c.ap(), "ttok")):
            sc.dma("cst", lambda e, d=dst, s=src: e.dma_start(out=d[:], in_=s),
                   writes=[key])
        sc.op("vector", lambda e: e.memset(eps_sb[:], EPS), writes=["eps"])

        sc.op("vector", lambda e: e.tensor_tensor(
            O_all.rearrange("p (t e) -> p t e", e=8),
            iotat_sb[:].unsqueeze(1).broadcast_to((128, NSUB, 8)),
            eidx_sb[:].unsqueeze(2).broadcast_to((128, NSUB, 8)), ALU.is_equal),
            reads=["iotat", "eidx"], writes=["O_all"])
        sc.op("tensor", lambda e: e.matmul(
            PSM[0:1, 0:8], cmask_sb[:], cnts_sb[:], start=True, stop=True,
            skip_group_check=True),
            reads=["cmask", "cnts"], writes=["PSM"])
        sc.op("vector", lambda e: e.tensor_copy(co_row[:], PSM[0:1, 0:8]),
              reads=["PSM"], writes=["corow"])
        sc.op("tensor", lambda e: e.matmul(
            PSM[:, 8:16], onerf_sb[:], co_row[:], start=True, stop=True,
            skip_group_check=True),
            reads=["onerf", "corow", "PSM"], writes=["PSM"])
        sc.op("vector", lambda e: e.tensor_copy(co_bc[:], PSM[:, 8:16]),
              reads=["PSM"], writes=["cobc"])
        sc.op("tensor", lambda e: e.matmul(
            PSM[:, 16:24], onerf_sb[:], offr_sb[:], start=True, stop=True,
            skip_group_check=True),
            reads=["onerf", "offr", "PSM"], writes=["PSM"])
        sc.op("vector", lambda e: e.tensor_copy(off_bc[:], PSM[:, 16:24]),
              reads=["PSM"], writes=["offbc"])
        sc.op("vector", lambda e: e.tensor_tensor(
            t1_sb.rearrange("p (t e) -> p t e", e=8),
            O_all.rearrange("p (t e) -> p t e", e=8),
            co_bc[:].unsqueeze(1).broadcast_to((128, NSUB, 8)), ALU.mult),
            reads=["O_all", "cobc"], writes=["t1"])
        sc.op("vector", lambda e: e.reduce_sum(
            co_own[:].rearrange("p (t o) -> p t o", o=1),
            t1_sb.rearrange("p (t e) -> p t e", e=8), AX.X),
            reads=["t1"], writes=["coown"])
        sc.op("vector", lambda e: e.tensor_tensor(rankg_sb[:], rankl_sb[:], co_own[:], ALU.add),
              reads=["rankl", "coown"], writes=["rankg"])
        sc.op("vector", lambda e: e.tensor_single_scalar(keep_sb[:], rankg_sb[:], float(T), ALU.is_lt),
              reads=["rankg"], writes=["keep"])
        sc.op("vector", lambda e: e.tensor_single_scalar(nk_i[:], rankg_sb[:], float(T), ALU.is_ge),
              reads=["rankg"], writes=["nk"])
        sc.op("vector", lambda e: e.tensor_tensor(s_all[:], keep_sb[:], p_sb[:], ALU.mult),
              reads=["keep", "p"], writes=["sall"])
        sc.op("vector", lambda e: e.tensor_tensor(
            O_kept.rearrange("p (t e) -> p t e", e=8),
            O_all.rearrange("p (t e) -> p t e", e=8),
            keep_sb[:].unsqueeze(2).broadcast_to((128, NSUB, 8)), ALU.mult),
            reads=["O_all", "keep"], writes=["O_kept"])
        sc.op("tensor", lambda e: e.matmul(
            PH0[:, 0:512], l128_sb[:], O_kept, start=True, stop=False,
            skip_group_check=True),
            reads=["l128", "O_kept"], writes=["PH0"])
        sc.op("tensor", lambda e: e.matmul(
            PSM[0:1, 24:536] if False else PSM[0:1, 0:512], onec_sb[:], O_kept,
            start=True, stop=True, skip_group_check=True),
            reads=["onec", "O_kept", "PSM"], writes=["PSM"])
        sc.op("vector", lambda e: e.tensor_copy(cnt_row2, PSM[0:1, 0:512]),
              reads=["PSM"], writes=["cntrow2"])
        sc.dma("bnc", lambda e: e.dma_start(
            out=cnt2_dram.ap().rearrange("a b -> (a b)")[None, :], in_=cnt_row2),
            reads=["cntrow2"], writes=["cntd2"])
        sc.dma("bnc", lambda e: e.dma_start(out=cnt64_2[:], in_=cnt2_dram.ap()),
               reads=["cntd2"], writes=["cnt642"])
        sc.op("tensor", lambda e: e.matmul(
            PSM[0:64, 24:32], l128_sb[0:64, 0:64], cnt64_2[:], start=True, stop=True,
            skip_group_check=True),
            reads=["l128", "cnt642", "PSM"], writes=["PSM"])
        sc.op("vector", lambda e: e.tensor_copy(to64_2[:], PSM[0:64, 24:32]),
              reads=["PSM"], writes=["to642"])
        sc.dma("bnc", lambda e: e.dma_start(out=to2_dram.ap(), in_=to64_2[:]),
               reads=["to642"], writes=["tod2"])
        sc.dma("bnc", lambda e: e.dma_start(
            out=to_row2, in_=to2_dram.ap().rearrange("a b -> (a b)")[None, :]),
            reads=["tod2"], writes=["torow2"])
        sc.op("tensor", lambda e: e.matmul(
            PH0[:, 0:512], onerf_sb[:], to_row2, start=False, stop=True,
            skip_group_check=True),
            reads=["onerf", "torow2", "PH0"], writes=["PH0"])
        sc.op("vector", lambda e: e.tensor_copy(RK_all, PH0[:, 0:512]),
              reads=["PH0"], writes=["RK"])
        sc.op("vector", lambda e: e.tensor_tensor(t1_sb, RK_all, O_kept, ALU.mult),
              reads=["RK", "O_kept"], writes=["t1"])
        sc.op("vector", lambda e: e.reduce_sum(
            krank_sb[:].rearrange("p (t o) -> p t o", o=1),
            t1_sb.rearrange("p (t e) -> p t e", e=8), AX.X),
            reads=["t1"], writes=["krank"])
        sc.op("vector", lambda e: e.tensor_tensor(
            t1_sb.rearrange("p (t e) -> p t e", e=8),
            O_all.rearrange("p (t e) -> p t e", e=8),
            off_bc[:].unsqueeze(1).broadcast_to((128, NSUB, 8)), ALU.mult),
            reads=["O_all", "offbc"], writes=["t1"])
        sc.op("vector", lambda e: e.reduce_sum(
            offown_sb[:].rearrange("p (t o) -> p t o", o=1),
            t1_sb.rearrange("p (t e) -> p t e", e=8), AX.X),
            reads=["t1"], writes=["offown"])
        sc.op("vector", lambda e: e.tensor_tensor(slotf_sb[:], offown_sb[:], krank_sb[:], ALU.add),
              reads=["offown", "krank"], writes=["slotf"])
        sc.op("vector", lambda e: e.select(slotd_f[:], nk_i[:], trash_sb[:], slotf_sb[:],
                                           add_drain=True),
              reads=["nk", "trash", "slotf"], writes=["slotdf"])
        sc.op("vector", lambda e: e.tensor_copy(slotd_i[:], slotd_f[:]),
              reads=["slotdf"], writes=["slotdi"])
        sc.op("vector", lambda e: e.select(slotf_sb[:], nk_i[:], ttok_sb[:], slotf_sb[:],
                                           add_drain=True),
              reads=["nk", "ttok", "slotf"], writes=["slotf"])
        sc.op("vector", lambda e: e.tensor_copy(slot_i[:], slotf_sb[:]),
              reads=["slotf"], writes=["sloti"])

        sc.op("vector", lambda e: e.memset(zrow_bf, 0.0), writes=["zrow"])
        ntile_all = TOT // 128
        sc.dma("zzf", lambda e, n=ntile_all: e.dma_start(
            out=gidx_dram.ap()[0:TOT, :].rearrange("(a b) o -> a (b o)", a=128),
            in_=zrow_bf[:, 0:2 * n].bitcast(I32)),
            reads=["zrow"], writes=["gidx0"])
        sc.dma("zzf", lambda e: e.dma_start(
            out=gidx_dram.ap()[TOT:TOT + 1, :],
            in_=zrow_bf[0:1, 0:2].bitcast(I32)),
            reads=["zrow"], writes=["gidx0b"])
        sc.dma("zzf", lambda e: e.dma_start(
            out=ye_buf.ap()[TOT:TOT + 1, :], in_=zrow_bf[0:1, 0:DM]),
            reads=["zrow"], writes=["yetrash"])

        for fg in range(2):
            wb = fg % 2
            for dh in range(2):
                sc.dma("w1", lambda e, fg=fg, wb=wb, dh=dh: e.dma_start(
                    out=w1g_bf[:, wb * 4096 + dh * 2048:
                               wb * 4096 + (dh + 1) * 2048].rearrange(
                        "p (dc f) -> p dc f", dc=4),
                    in_=w1_in[0].rearrange("(dc p) f -> p dc f", dc=8)
                        [:, dh * 4:(dh + 1) * 4, fg * 512:(fg + 1) * 512]),
                    writes=[f"w1g{wb}"] if dh == 1 else [f"w1g{wb}h"])
        for fc in range(32):
            sc.dma("w2", lambda e, fc=fc: e.dma_start(
                out=w2_sb[:, fc * 1024:(fc + 1) * 1024],
                in_=w2_in[0, fc * 128:(fc + 1) * 128, :]),
                writes=[f"w2{fc}"])

        sct_keys = []
        for stt in range(NSUB):
            sct_keys.append(sc.dma("sct", lambda e, stt=stt: e.indirect_dma_start(
                out=gidx_dram.ap(), out_offset=IndirectOffsetOnAxis(slot_i[:, stt:stt + 1], 0),
                in_=tokid_sb[:, stt:stt + 1], in_offset=None),
                reads=["tokid", "sloti"] + (["gidx0"] if stt == 0 else []),
                writes=[f"sctw{stt}"]))
        for g in range(TOT // 128):
            sc.dma("xsc", lambda e, g=g: e.dma_start(
                out=gidx_sb[:, g:g + 1],
                in_=gidx_dram.ap()[g * 128:(g + 1) * 128, :]),
                extra=(sct_keys if g == 0 else []), writes=[f"gidx{g}"])

        for ex in range(NE):
            chunks = []
            t0 = 0
            while t0 < nt[ex]:
                tl = min(8, nt[ex] - t0)
                chunks.append((t0, tl))
                t0 += tl
            for ci, (ct0, tl) in enumerate(chunks):
                gbase = offt[ex] // 128 + ct0
                for ti in range(tl):
                    sl = ti % 2
                    gg = gbase + ti
                    sc.dma("xet", lambda e, g=gg, sl=sl: e.indirect_dma_start(
                        out=xet_bf[:, sl * DM:(sl + 1) * DM], out_offset=None,
                        in_=x1b_in, in_offset=IndirectOffsetOnAxis(gidx_sb[:, g:g + 1], 0)),
                        reads=[f"gidx{gg}"], writes=[f"xet{sl}"])
                    for dc in range(8):
                        sc.op("tensor", lambda e, dc=dc, sl=sl: e.transpose(
                            TXC[:, dc * 128:(dc + 1) * 128],
                            xet_bf[:, sl * DM + dc * 128: sl * DM + (dc + 1) * 128],
                            idb_sb[:]),
                            reads=[f"xet{sl}", "idb"],
                            writes=["TXC"] if dc in (0, 7) else [])
                    sc.op("vector", lambda e, ti=ti: e.tensor_copy(
                        xe_fm[:].rearrange("p (dc t) -> p dc t", dc=8)[:, :, ti * 128:(ti + 1) * 128],
                        TXC[:].rearrange("p (dc t) -> p dc t", dc=8)),
                        reads=["TXC"], writes=[f"xefm{ti}"])
                xefm_keys = [f"xefm{ti}" for ti in range(tl)]
                nsl_total = tl * 128
                sgs = []
                o = 0
                while o < nsl_total:
                    sgs.append((o, min(512, nsl_total - o)))
                    o += 512

                for fg in range(8):
                    wb = fg % 2
                    if ex == 0 and ci == 0 and fg < 2:
                        pass
                    else:
                      for dh in range(2):
                        sc.dma("w1", lambda e, ex=ex, fg=fg, wb=wb, dh=dh: e.dma_start(
                            out=w1g_bf[:, wb * 4096 + dh * 2048:
                                       wb * 4096 + (dh + 1) * 2048].rearrange(
                                "p (dc f) -> p dc f", dc=4),
                            in_=w1_in[ex].rearrange("(dc p) f -> p dc f", dc=8)
                                [:, dh * 4:(dh + 1) * 4, fg * 512:(fg + 1) * 512]),
                            writes=[f"w1g{wb}"] if dh == 1 else [f"w1g{wb}h"])
                    for fsub in range(4):
                        fc = fg * 4 + fsub
                        for sgi, (so, sn) in enumerate(sgs):
                            ph = PH0 if (fc * len(sgs) + sgi) % 2 == 0 else PH1
                            phk = "PH0" if (fc * len(sgs) + sgi) % 2 == 0 else "PH1"
                            for dc in range(8):
                                sc.op("tensor",
                                      lambda e, wb=wb, fsub=fsub, dc=dc, so=so, sn=sn, ph=ph:
                                      e.matmul(
                                          ph[:, 0:sn],
                                          w1g_bf[:, wb * 4096 + dc * 512 + fsub * 128:
                                                 wb * 4096 + dc * 512 + (fsub + 1) * 128],
                                          xe_fm[:, dc * 1024 + so: dc * 1024 + so + sn],
                                          start=(dc == 0), stop=(dc == 7)),
                                      reads=[f"w1g{wb}", f"w1g{wb}h"] +
                                            (xefm_keys if dc in (0, 7) else []),
                                      writes=[phk] if dc in (0, 7) else [])
                            if (fc * len(sgs) + sgi) % 2 == 0:
                                sc.op("scalar", lambda e, fc=fc, so=so, sn=sn, ph=ph: e.activation(
                                    h_sb[:, fc * 1024 + so: fc * 1024 + so + sn],
                                    ph[:, 0:sn], AF.Relu),
                                    reads=[phk],
                                    writes=[f"h{fc}_{sgi}"] +
                                           (ALIAS_KEYS if (ex == 0 and ci == 0 and fc == 0 and sgi == 0)
                                            else []))
                            else:
                                sc.op("vector", lambda e, fc=fc, so=so, sn=sn, ph=ph:
                                      e.tensor_scalar_max(
                                          h_sb[:, fc * 1024 + so: fc * 1024 + so + sn],
                                          ph[:, 0:sn], 0.0),
                                      reads=[phk], writes=[f"h{fc}_{sgi}"])
                h_keys = [f"h{fc}_{sgi}" for fc in range(32) for sgi in range(len(sgs))]

                if ci == 0 and ex > 0:
                    for fc in range(32):
                        sc.dma("w2", lambda e, ex=ex, fc=fc: e.dma_start(
                            out=w2_sb[:, fc * 1024:(fc + 1) * 1024],
                            in_=w2_in[ex, fc * 128:(fc + 1) * 128, :]),
                            writes=[f"w2{fc}"])

                for ti in range(tl):
                    py = PY0 if ti % 2 == 0 else PY1
                    pyk = "PY0" if ti % 2 == 0 else "PY1"
                    for fc in range(32):
                        for half in range(2):
                            first = (fc == 0 and half == 0)
                            last = (fc == 31 and half == 1)
                            sc.op("tensor", lambda e, ti=ti, fc=fc, half=half, py=py: e.matmul(
                                py[:, half * 512:(half + 1) * 512],
                                h_sb[:, fc * 1024 + ti * 128: fc * 1024 + (ti + 1) * 128],
                                w2_sb[:, fc * 1024 + half * 512: fc * 1024 + (half + 1) * 512],
                                start=(fc == 0), stop=(fc == 31)),
                                reads=[f"w2{fc}", f"h{fc}_{(ti * 128) // 512}"] +
                                      (h_keys if first else []),
                                writes=[pyk] if (first or last) else [])
                    sl = ti % 2
                    for half in range(2):
                        sc.op("vector", lambda e, sl=sl, half=half, py=py: e.tensor_copy(
                            yet_bf[:, sl * DM + half * 512: sl * DM + (half + 1) * 512],
                            py[:, half * 512:(half + 1) * 512]),
                            reads=[pyk], writes=[f"yet{sl}_{half}"])
                    gg = gbase + ti
                    sc.dma("yeo", lambda e, g=gg, sl=sl: e.dma_start(
                        out=ye_buf.ap()[g * 128:(g + 1) * 128, :],
                        in_=yet_bf[:, sl * DM:(sl + 1) * DM]),
                        reads=[f"yet{sl}_0", f"yet{sl}_1"], writes=[f"yeo{gg}"])

        ye_keys = [f"yeo{g}" for g in range(TOT // 128)] + ["yetrash"]
        ye_keys_tok = [sc.wr[k] for k in ye_keys if k in sc.wr]

        t_outs = []
        for stt in range(NSUB):
            sl = stt % 3
            lb = 8 * sl
            sc.dma("yg", lambda e, stt=stt, sl=sl: e.indirect_dma_start(
                out=yg_bf[:, sl * DM:(sl + 1) * DM], out_offset=None,
                in_=ye_buf.ap(), in_offset=IndirectOffsetOnAxis(slotd_i[:, stt:stt + 1], 0)),
                reads=["slotdi"] + (ye_keys if stt == 0 else []),
                writes=[f"yg{sl}"])
            sc.dma("x1d", lambda e, stt=stt, sl=sl: e.dma_start(
                out=x1d_f[:, sl * DM:(sl + 1) * DM],
                in_=x1f_in[stt * 128:(stt + 1) * 128, :]),
                writes=[f"x1d{sl}"])
            sc.op("vector", lambda e, stt=stt, sl=sl, lb=lb: e.scalar_tensor_tensor(
                y2_f[:, sl * DM:(sl + 1) * DM], yg_bf[:, sl * DM:(sl + 1) * DM],
                s_all[:, stt:stt + 1], x1d_f[:, sl * DM:(sl + 1) * DM],
                ALU.mult, ALU.add, accum_out=ln_sm[:, lb + 0:lb + 1]),
                reads=[f"yg{sl}", f"x1d{sl}", "sall"],
                writes=[f"y2{sl}", f"lnsum{sl}"])
            sc.op("scalar", lambda e, sl=sl, lb=lb: e.activation(
                sq_scr[:, 0:DM], y2_f[:, sl * DM:(sl + 1) * DM], AF.Square,
                accum_out=ln_sm[:, lb + 1:lb + 2]),
                reads=[f"y2{sl}"], writes=[f"lnsq{sl}", "sq_alias"])
            sc.op("vector", lambda e, lb=lb: e.tensor_scalar_mul(
                ln_sm[:, lb + 2:lb + 3], ln_sm[:, lb + 0:lb + 1], 1.0 / DM),
                reads=[f"lnsum{sl}"], writes=[f"lnm{sl}"])
            sc.op("vector", lambda e, lb=lb: e.tensor_tensor(
                ln_sm[:, lb + 3:lb + 4], ln_sm[:, lb + 2:lb + 3], ln_sm[:, lb + 2:lb + 3],
                ALU.mult),
                reads=[f"lnm{sl}"], writes=[f"lnm2{sl}"])
            sc.op("vector", lambda e, lb=lb: e.scalar_tensor_tensor(
                ln_sm[:, lb + 4:lb + 5], ln_sm[:, lb + 1:lb + 2], 1.0 / DM,
                ln_sm[:, lb + 3:lb + 4], ALU.mult, ALU.subtract),
                reads=[f"lnsq{sl}", f"lnm2{sl}"], writes=[f"lnvar{sl}"])
            sc.op("scalar", lambda e, lb=lb: e.activation(
                ln_sm[:, lb + 5:lb + 6], ln_sm[:, lb + 4:lb + 5], AF.Ln, bias=eps_sb[:], scale=1.0),
                reads=[f"lnvar{sl}", "eps"], writes=[f"lnln{sl}"])
            sc.op("scalar", lambda e, lb=lb: e.activation(
                ln_sm[:, lb + 6:lb + 7], ln_sm[:, lb + 5:lb + 6], AF.Exp, bias=0.0, scale=-0.5),
                reads=[f"lnln{sl}"], writes=[f"lnrs{sl}"])
            sc.op("vector", lambda e, lb=lb: e.scalar_tensor_tensor(
                ln_sm[:, lb + 7:lb + 8], ln_sm[:, lb + 2:lb + 3], -1.0,
                ln_sm[:, lb + 6:lb + 7], ALU.mult, ALU.mult),
                reads=[f"lnm{sl}", f"lnrs{sl}"], writes=[f"lnmb{sl}"])
            sc.op("vector", lambda e, sl=sl, lb=lb: e.tensor_scalar(
                outt_f[:, sl * DM:(sl + 1) * DM], y2_f[:, sl * DM:(sl + 1) * DM],
                ln_sm[:, lb + 6:lb + 7], ln_sm[:, lb + 7:lb + 8],
                ALU.mult, ALU.add),
                reads=[f"y2{sl}", f"lnmb{sl}", f"lnrs{sl}"],
                writes=[f"outt{sl}", f"x1d{sl}"])
            t_outs.append(sc.dma("out", lambda e, stt=stt, sl=sl: e.dma_start(
                out=out_o[stt * 128:(stt + 1) * 128, :],
                in_=outt_f[:, sl * DM:(sl + 1) * DM]),
                reads=[f"outt{sl}", f"x1d{sl}"], writes=[f"outo{sl}"]))

        fin = list(t_outs)
        for stn in ("out", "yeo"):
            fin += sc.total(stn)
        fin += [("eng:scalar", sc.cnt["eng:scalar"]), ("eng:gpsimd", sc.cnt["eng:gpsimd"]),
                ("eng:vector", sc.cnt["eng:vector"]), ("eng:tensor", sc.cnt["eng:tensor"])]
        sc.op("sync", lambda e: e.nop(), extra=fin)

        sc.emit(block)
    return nc



_CACHE = {}


def _maskbias(bias_table):
    ws = WS
    coords = np.stack(np.meshgrid(np.arange(ws), np.arange(ws), indexing="ij"))
    cf = coords.reshape(2, -1)
    rel = cf[:, :, None] - cf[:, None, :]
    rel = rel.transpose(1, 2, 0).astype(np.int64)
    rel[..., 0] += ws - 1
    rel[..., 1] += ws - 1
    rel[..., 0] *= 2 * ws - 1
    idx = rel.sum(-1)[:ws, :ws]
    bias = np.asarray(bias_table, dtype=np.float32)[idx.reshape(-1)].reshape(ws, ws, H)
    mb = np.full((H, 128, 32), -1e9, dtype=np.float32)
    for h in range(H):
        for p in range(128):
            qw = (p % 32) // 8
            qi = p % 8
            mb[h, p, qw * 8:(qw + 1) * 8] = bias[qi, :, h]
    mb0 = np.concatenate([mb[2 * hp] for hp in range(8)], axis=1)
    mb1 = np.concatenate([mb[2 * hp + 1] for hp in range(8)], axis=1)
    return np.ascontiguousarray(mb0), np.ascontiguousarray(mb1)


def kernel(L0_attn_x, L0_attn_w_qkv, L0_attn_b_qkv, L0_attn_w_proj, L0_attn_b_proj,
           L0_attn_bias_table, L0_ffn_Wr, L0_ffn_W1, L0_ffn_W2):
    x = np.ascontiguousarray(np.asarray(L0_attn_x, dtype=np.float32))
    wqkv = np.ascontiguousarray(np.asarray(L0_attn_w_qkv, dtype=np.float32))
    bqkv = np.ascontiguousarray(np.asarray(L0_attn_b_qkv, dtype=np.float32))
    wproj = np.ascontiguousarray(np.asarray(L0_attn_w_proj, dtype=np.float32))
    bproj = np.ascontiguousarray(np.asarray(L0_attn_b_proj, dtype=np.float32))
    wr = np.ascontiguousarray(np.asarray(L0_ffn_Wr, dtype=np.float32))
    w1 = np.ascontiguousarray(np.asarray(L0_ffn_W1, dtype=np.float32))
    w2 = np.ascontiguousarray(np.asarray(L0_ffn_W2, dtype=np.float32))
    mb0, mb1 = _maskbias(L0_attn_bias_table)

    if "l1" not in _CACHE:
        _CACHE["l1"] = build_l1()
    nc1 = _CACHE["l1"]
    in_maps1 = []
    for c in range(NCORES):
        in_maps1.append({
            "x": np.ascontiguousarray(x[c * NSEQ:(c + 1) * NSEQ].reshape(T, DM)),
            "wqkv": wqkv, "bqkv": bqkv, "wproj": wproj, "bproj": bproj,
            "mb0": mb0, "mb1": mb1, "wr": wr,
        })
    res1 = run_bass_kernel_spmd(nc1, in_maps1, core_ids=list(range(NCORES)))

    cnts = np.ascontiguousarray(
        np.stack([np.asarray(res1.results[c]["tot"][0]) for c in range(NCORES)]).astype(np.float32))

    prefix = np.cumsum(cnts, axis=0) - cnts
    kept = np.clip(np.minimum(cnts, 8192.0 - prefix), 0.0, None)
    nt = tuple(int(max(1, int(np.ceil(kept[:, e].max() / 128.0)))) for e in range(NE))
    if ("l2", nt) not in _CACHE:
        _CACHE[("l2", nt)] = build_l2(nt)
    nc2 = _CACHE[("l2", nt)]
    in_maps2 = []
    for c in range(NCORES):
        r = res1.results[c]
        cmask = (np.arange(NE) < c).astype(np.float32).reshape(NE, 1)
        in_maps2.append({
            "x1f": np.ascontiguousarray(r["x1f"]),
            "x1b": np.ascontiguousarray(r["x1b"]),
            "eidx": np.ascontiguousarray(r["eidx"]),
            "ptok": np.ascontiguousarray(r["ptok"]),
            "rankl": np.ascontiguousarray(r["rankl"]),
            "cnts": cnts, "cmask": cmask, "w1": w1, "w2": w2,
        })
    res2 = run_bass_kernel_spmd(nc2, in_maps2, core_ids=list(range(NCORES)))

    out = np.concatenate(
        [np.asarray(res2.results[c]["out"]).reshape(NSEQ, SEQ, DM) for c in range(NCORES)],
        axis=0)
    return np.ascontiguousarray(out.astype(np.float32))
